# version 14
# baseline (speedup 1.0000x reference)
import math
from contextlib import ExitStack

import numpy as np
import ml_dtypes

import concourse.bass as bass
import concourse.mybir as mybir
from concourse.bass_utils import run_bass_kernel_spmd

F32 = mybir.dt.float32
BF16 = mybir.dt.bfloat16
E4 = mybir.dt.float8e4
AF = mybir.ActivationFunctionType
OP = mybir.AluOpType
PM = mybir.MatmulPerfMode

T = 2048
D = 2048
NHEAD = 16
DH = 128
FF = 8192
NC_D = D // 128
NC_T = T // 128
NC_F = FF // 128
THALF = T // 2

C1 = float(np.float32(0.044715).astype(ml_dtypes.bfloat16))
C2 = float(np.float32(np.sqrt(2.0 / np.pi)).astype(ml_dtypes.bfloat16))
RSQRT_DH = 1.0 / math.sqrt(128.0)
EPS = 1e-5
NEG = -1.0e9

ENGINES = ("sync", "gpsimd", "tensor", "vector", "scalar")


class Ctr:

    def __init__(self, h):
        self.h = h
        self.n = 0


class Emitter:
    def __init__(self):
        self.streams = {e: [] for e in ENGINES}
        self.waited = {e: {} for e in ENGINES}

    def op(self, eng, fn, waits=(), inc=None, incn=1):
        w = []
        seen = self.waited[eng]
        for ctr, val in waits:
            if val is None:
                val = ctr.n
            key = id(ctr)
            if seen.get(key, -1) < val:
                seen[key] = val
                w.append((ctr.h, val))
        mark = None
        if inc is not None:
            inc.n += incn
            mark = inc.n
        self.streams[eng].append((fn, w, (inc.h if inc else None), incn))
        return mark

    def replay(self, eng_name, eng):
        for fn, waits, inc_h, incn in self.streams[eng_name]:
            for sem, val in waits:
                eng.wait_ge(sem, val)
            ins = fn(eng)
            if inc_h is not None:
                ins.then_inc(inc_h, incn)


def build_program(taps=(), phases=3, ablate_stats=False):
    nc = bass.Bass("TRN2", target_bir_lowering=False)
    x_d = nc.dram_tensor("x", [T, D], F32, kind="ExternalInput").ap()
    wq_d = nc.dram_tensor("wq", [D, D], F32, kind="ExternalInput").ap()
    wk_d = nc.dram_tensor("wk", [D, D], F32, kind="ExternalInput").ap()
    wv_d = nc.dram_tensor("wv", [D, D], F32, kind="ExternalInput").ap()
    wo_d = nc.dram_tensor("wo", [D, D], F32, kind="ExternalInput").ap()
    bq_d = nc.dram_tensor("bq", [D], F32, kind="ExternalInput").ap()
    bk_d = nc.dram_tensor("bk", [D], F32, kind="ExternalInput").ap()
    bv_d = nc.dram_tensor("bv", [D], F32, kind="ExternalInput").ap()
    bo_d = nc.dram_tensor("bo", [D], F32, kind="ExternalInput").ap()
    wg_d = nc.dram_tensor("wg", [2, D, FF], F32, kind="ExternalInput").ap()
    wl_d = nc.dram_tensor("wl", [FF, D], F32, kind="ExternalInput").ap()
    ident_d = nc.dram_tensor("ident", [128, 128], F32, kind="ExternalInput").ap()
    identb_d = nc.dram_tensor("identb_in", [128, 128], BF16, kind="ExternalInput").ap()
    ones_d = nc.dram_tensor("ones_in", [128, 128], BF16, kind="ExternalInput").ap()
    mask_d = nc.dram_tensor("maskc", [4, 128, 512], F32, kind="ExternalInput").ap()
    out_d = nc.dram_tensor("out", [T, D], F32, kind="ExternalOutput").ap()
    tap_outs = {}
    for name, shape, dt_ in taps:
        tap_outs[name] = nc.dram_tensor("tap_" + name, shape, dt_, kind="ExternalOutput").ap()

    E = Emitter()
    es = ExitStack()

    def sb(name, shape, dt_, side="left"):
        return es.enter_context(nc.sbuf_tensor(name, shape, dt_, side=side))

    def ps(name, shape, dt_=F32):
        return es.enter_context(nc.psum_tensor(name, shape, dt_))

    def sem(name):
        return Ctr(es.enter_context(nc.semaphore(name)))

    sx = sem("sx")
    sw = sem("sw")
    sg = sem("sg")
    st = sem("st")
    sv = sem("sv")
    sc = sem("sc")
    so = sem("so")
    sbm = sem("sbm")
    sx2 = sem("sx2")

    ident32 = sb("ident32", [128, 128], F32)
    bias_all = sb("bias_all", [128, 64], F32)
    xn_e4 = sb("xn_e4", [128, NC_D * T], E4)
    attn_h = sb("attn_h", [128, NHEAD * T], BF16)

    with es:
        marks = {}

        def init_phase():
            E.op("sync", lambda s: s.dma_start(out=ident32[:], in_=ident_d),
                 inc=sg, incn=16)
            E.op("sync", lambda s: s.dma_start(out=identb[:], in_=identb_d),
                 inc=sg, incn=16)
            E.op("sync", lambda s: s.dma_start(out=ones_bf[:], in_=ones_d),
                 inc=sg, incn=16)
            marks["ident32"] = sg.n
            marks["identb"] = sg.n

        init_phase()
        bias_loaded = None

        def init_masks():
            for m in range(4):
                E.op("gpsimd", lambda g, m=m: g.dma_start(
                    out=masks[m][:], in_=mask_d[m]),
                    inc=sg, incn=16)

        with ExitStack() as es_a:
            def sba(name, shape, dt_):
                return es_a.enter_context(
                    nc.sbuf_tensor(name, shape, dt_, side="right"))

            xn_bf = sba("xn_bf", [128, NC_D * T], BF16)
            ones_bf = sba("ones_bf", [128, 128], BF16)
            identb = sba("identb", [128, 128], BF16)
            bias_stage = sba("bias_stage", [64, 128], F32)
            masks = [sba(f"mask{m}", [128, 512], BF16) for m in range(4)]
            if phases >= 2:
                init_masks()
            masks_done = sg.n
            wqb = sba("wqb", [128, NC_D * 128], BF16)
            wkb = sba("wkb", [128, NC_D * 128], BF16)
            wvb = sba("wvb", [128, NC_D * 128], BF16)

            with ExitStack() as es0:
                def sb0(name, shape, dt_):
                    return es0.enter_context(
                        nc.sbuf_tensor(name, shape, dt_, side="right"))
                x_tm = [sb0(f"x_tm{i}", [128, D], F32) for i in range(2)]

                xn_tm = [sb0("xn_tm0", [128, D], F32) for i in range(1)]
                stat = [sb0(f"stat{i}", [128, 8], F32) for i in range(2)]
                with (nc.psum_tensor("psT0", [128, 512], F32) as psT0,
                      nc.psum_tensor("psT1", [128, 512], F32) as psT1):
                    psT = [psT0, psT1]
                    x_used = [0, 0]
                    xn_used = [0, 0]
                    evac_mark = {}
                    for i in range(NC_T):
                        b = i % 2
                        ld = E.op("sync", lambda s, i=i, b=b: s.dma_start(
                            out=x_tm[b][:], in_=x_d[i * 128:(i + 1) * 128, :]),
                            waits=[(sc, x_used[b]), (sx, sx.n)], inc=sx, incn=16)
                        ld2 = ld
                        E.op("vector", lambda v, b=b: v.reduce_sum(
                            stat[b][:, 0:1], x_tm[b][:], axis=mybir.AxisListType.X),
                            waits=[(sx, ld2)], inc=sv)
                        mk = sv.n
                        E.op("scalar", lambda c, b=b: c.activation(
                            xn_tm[0][:], x_tm[b][:], AF.Square),
                            waits=[(sx, ld2), (st, xn_used[0])], inc=sc)
                        sq_mark = sc.n
                        E.op("vector", lambda v, b=b: v.reduce_sum(
                            stat[b][:, 1:2], xn_tm[0][:],
                            axis=mybir.AxisListType.X),
                            waits=[(sc, sq_mark)], inc=sv)
                        sq_red = sv.n
                        E.op("vector", lambda v, b=b: v.tensor_scalar(
                            stat[b][:, 0:1], stat[b][:, 0:1], 1.0 / 2048.0, None,
                            OP.mult), waits=[(sv, mk)], inc=sv)
                        mk = sv.n
                        E.op("vector", lambda v, b=b: v.tensor_scalar(
                            stat[b][:, 1:2], stat[b][:, 1:2], 1.0 / 2048.0, None,
                            OP.mult), waits=[(sv, sq_red), (sv, mk)], inc=sv)
                        mk = sv.n
                        E.op("vector", lambda v, b=b: v.tensor_scalar(
                            stat[b][:, 2:3], stat[b][:, 0:1], stat[b][:, 0:1],
                            None, OP.mult), waits=[(sv, mk)], inc=sv)
                        mk = sv.n
                        E.op("vector", lambda v, b=b: v.tensor_scalar(
                            stat[b][:, 1:2], stat[b][:, 1:2], stat[b][:, 2:3],
                            EPS, OP.subtract, OP.add), waits=[(sv, mk)], inc=sv)
                        var_mark = sv.n
                        E.op("scalar", lambda c, b=b: c.activation(
                            stat[b][:, 3:4], stat[b][:, 1:2], AF.Sqrt),
                            waits=[(sv, var_mark)], inc=sc)
                        sqrt_mark = sc.n
                        E.op("vector", lambda v, b=b: v.reciprocal(
                            stat[b][:, 4:5], stat[b][:, 3:4]),
                            waits=[(sc, sqrt_mark)], inc=sv)
                        mk = sv.n
                        for _ in range(2):
                            E.op("vector", lambda v, b=b: v.tensor_scalar(
                                stat[b][:, 5:6], stat[b][:, 4:5], stat[b][:, 4:5],
                                None, OP.mult), waits=[(sv, mk)], inc=sv)
                            mk = sv.n
                            E.op("vector", lambda v, b=b: v.tensor_scalar(
                                stat[b][:, 5:6], stat[b][:, 5:6], stat[b][:, 1:2],
                                None, OP.mult), waits=[(sv, mk)], inc=sv)
                            mk = sv.n
                            E.op("vector", lambda v, b=b: v.tensor_scalar(
                                stat[b][:, 5:6], stat[b][:, 5:6], -0.5, 1.5,
                                OP.mult, OP.add), waits=[(sv, mk)], inc=sv)
                            mk = sv.n
                            E.op("vector", lambda v, b=b: v.tensor_scalar(
                                stat[b][:, 4:5], stat[b][:, 4:5], stat[b][:, 5:6],
                                None, OP.mult), waits=[(sv, mk)], inc=sv)
                            mk = sv.n
                        E.op("vector", lambda v, b=b: v.tensor_scalar(
                            stat[b][:, 6:7], stat[b][:, 0:1], stat[b][:, 4:5],
                            -1.0, OP.mult, OP.mult), waits=[(sv, mk)], inc=sv)
                        rstd_mark = sv.n
                        if ablate_stats:
                            E.op("scalar", lambda c, b=b: c.activation(
                                xn_tm[0][:], x_tm[b][:], AF.Identity,
                                bias=0.0, scale=1.0),
                                waits=[(sv, rstd_mark), (st, xn_used[0])],
                                inc=sc)
                        else:
                            E.op("scalar", lambda c, b=b: c.activation(
                                xn_tm[0][:], x_tm[b][:], AF.Identity,
                                bias=stat[b][:, 6:7], scale=stat[b][:, 4:5]),
                                waits=[(sv, rstd_mark), (st, xn_used[0])],
                                inc=sc)
                        xn_mark = sc.n
                        x_used[b] = sc.n
                        for jg in range(4):
                            pb = jg % 2
                            w_tr = [(sc, xn_mark), (sg, marks["ident32"])]
                            if (i, jg) in evac_mark or ((i * 4 + jg) >= 2):
                                prev = evac_mark.get((i, jg - 2) if jg >= 2
                                                    else (i - 1, jg + 2), 0)
                                w_tr.append((sv, prev))
                            for q in range(4):
                                cchunk = 4 * jg + q
                                E.op("tensor", lambda t, b=b, pb=pb, q=q,
                                     cchunk=cchunk: t.transpose(
                                         psT[pb][:, q * 128:(q + 1) * 128],
                                         xn_tm[0][:, cchunk * 128:(cchunk + 1) * 128],
                                         ident32[:]),
                                     waits=w_tr if q == 0 else (),
                                     inc=(st if q == 3 else None))
                            tr_mark = st.n
                            def f_evbf(v, i=i, jg=jg, pb=pb):
                                o = xn_bf[:].rearrange("p (c t) -> p c t", t=T)[
                                    :, 4 * jg:4 * jg + 4, i * 128:(i + 1) * 128]
                                s_ = psT[pb][:].rearrange("p (c t) -> p c t", t=128)
                                return v.tensor_copy(o, s_)
                            E.op("vector", f_evbf, waits=[(st, tr_mark)])
                            def f_ev4(v, i=i, jg=jg, pb=pb):
                                o = xn_e4[:].rearrange("p (c t) -> p c t", t=T)[
                                    :, 4 * jg:4 * jg + 4, i * 128:(i + 1) * 128]
                                s_ = psT[pb][:].rearrange("p (c t) -> p c t", t=128)
                                return v.tensor_copy(o, s_)
                            E.op("vector", f_ev4, inc=sv)
                            evac_mark[(i, jg)] = sv.n
                        xn_used[0] = st.n
                        marks[f"p0_tile{i}"] = sv.n
                        if i == 1:
                            marks["early_tiles"] = sv.n
                    phase0_v = sv.n
                    phase0_c = sc.n
                    for mi, bsrc in enumerate((bq_d, bk_d, bv_d, bo_d)):
                        def f_b(s, bsrc=bsrc, mi=mi):
                            return s.dma_start(
                                out=bias_stage[mi * 16:(mi + 1) * 16, :],
                                in_=bsrc.rearrange("(h p) -> h p", p=128))
                        E.op("sync", f_b, inc=sbm, incn=16)
                    with nc.psum_tensor("psBias", [128, 64], F32) as psB:
                        E.op("tensor", lambda t: t.transpose(
                            psB[:], bias_stage[:, :], ident32[0:64, 0:64]),
                            waits=[(sbm, sbm.n), (sg, marks["ident32"])], inc=st)
                        bias_tr = st.n
                        E.op("vector", lambda v: v.tensor_copy(
                            bias_all[:], psB[:]),
                            waits=[(st, bias_tr)], inc=sv)
                    bias_loaded = sv.n

            def emit_tap(name, tens, waits):
                if name in tap_outs:
                    E.op("sync", lambda s, tens=tens, name=name: s.dma_start(
                        out=tap_outs[name], in_=tens[:]), waits=waits,
                        inc=so, incn=16)
            emit_tap("xn_bf", xn_bf, [(sv, phase0_v), (sc, phase0_c)])
            emit_tap("xn_e4", xn_e4, [(sv, phase0_v), (sc, phase0_c)])

            q_h = sba("q_h", [128, T], BF16)
            k_h = sba("k_h", [128, T], BF16)
            v_tm = sba("v_tm", [128, NC_T * 128], BF16)
            v_tmp = sba("v_tmp", [128, 512], BF16)
            pt_sb = [sba(f"pt{j}", [128, 1024], BF16) for j in range(2)]
            rd_sb = sba("rd_sb", [128, 512], F32)
            u_sb = sba("u_sb", [128, 512], F32)

            with (nc.psum_tensor("psP", [128, 512], F32) as psP,
                  nc.psum_tensor("psVt", [128, 512], BF16) as psVt,
                  nc.psum_tensor("psS0", [128, 1024], F32) as psS0,
                  nc.psum_tensor("psS1", [128, 1024], F32) as psS1,
                  nc.psum_tensor("psO", [128, 512], F32) as psO,
                  nc.psum_tensor("psD", [128, 512], F32) as psD):
                psS = [psS0, psS1]

                def w_load(g_dst, src_d, h, wait_mark):
                    def f(g, g_dst=g_dst, src_d=src_d, h=h):
                        inap = src_d.rearrange("(c p) m -> p c m", p=128)[
                            :, :, h * 128:(h + 1) * 128]
                        oap = g_dst[:].rearrange("p (c m) -> p c m", m=128)
                        return g.dma_start(out=oap, in_=inap)
                    return E.op("gpsimd", f, waits=[(st, wait_mark)],
                                inc=sw, incn=16)

                E.op("gpsimd", lambda g: g.engine_nop(),
                     waits=[(sv, marks.get("early_tiles", 0))])
                wq_ld = w_load(wqb, wq_d, 0, 0)
                wk_ld = w_load(wkb, wk_d, 0, 0)
                wv_ld = w_load(wvb, wv_d, 0, 0)
                proj_done = {"q": 0, "k": 0, "v": 0}
                evacP_mark = 0
                psO_free = 0
                psD_free = 0
                exp_consumed = {}
                pt_consumed = {}
                attn_written = 0

                psVt_copy = 0
                for h in range(NHEAD if phases >= 2 else 0):
                    for kind, wbuf, wld_mark, bias in (
                            ("q", wqb, wq_ld, 0),
                            ("k", wkb, wk_ld, 16),
                            ("v", wvb, wv_ld, None)):
                        for tc in range(4):
                            for c in range(NC_D):
                                w_ = [(sw, wld_mark)] if c == 0 and tc == 0 else []
                                if c == 0:
                                    w_.append((sc, evacP_mark))
                                    w_.append(
                                        (sv, marks[f"p0_tile{4 * tc + 3}"]))
                                E.op("tensor", lambda t, wbuf=wbuf, c=c, tc=tc:
                                     t.matmul(psP[:],
                                              wbuf[:, c * 128:(c + 1) * 128],
                                              xn_bf[:, c * T + tc * 512:
                                                    c * T + tc * 512 + 512],
                                              start=(c == 0), stop=(c == NC_D - 1),
                                              skip_group_check=True),
                                     waits=w_,
                                     inc=(st if c == NC_D - 1 else None))
                                if c == NC_D - 1:
                                    mm_mark = st.n
                            if kind in ("q", "k"):
                                dest = q_h if kind == "q" else k_h
                                E.op("scalar", lambda c_, dest=dest, tc=tc,
                                     bias=bias, h=h: c_.activation(
                                         dest[:, tc * 512:(tc + 1) * 512], psP[:],
                                         AF.Identity, bias=bias_all[:, bias + h:bias + h + 1],
                                         scale=1.0),
                                     waits=[(st, mm_mark)], inc=sc)
                                evacP_mark = sc.n
                            else:
                                E.op("scalar", lambda c_, tc=tc: c_.activation(
                                    v_tmp[:], psP[:], AF.Identity),
                                    waits=[(st, mm_mark)], inc=sc)
                                evacP_mark = sc.n
                                vv_mark = sc.n
                                for tb in range(4):
                                    E.op("tensor", lambda t, tb=tb: t.transpose(
                                        psVt[:, tb * 128:(tb + 1) * 128],
                                        v_tmp[:, tb * 128:(tb + 1) * 128],
                                        identb[:]),
                                        waits=([(sc, vv_mark), (sv, psVt_copy),
                                                (sv, marks["identb"])]
                                               if tb == 0 else ()),
                                        inc=(st if tb == 3 else None))
                                vt_mark = st.n
                                E.op("vector", lambda v, tc=tc: v.tensor_copy(
                                    v_tm[:, tc * 512:(tc + 1) * 512], psVt[:]),
                                    waits=[(st, vt_mark)], inc=sv)
                                psVt_copy = sv.n
                        proj_done[kind] = st.n
                        if h + 1 < NHEAD:
                            if kind == "q":
                                wq_ld = w_load(wqb, wq_d, h + 1, proj_done["q"])
                            elif kind == "k":
                                wk_ld = w_load(wkb, wk_d, h + 1, proj_done["k"])
                            else:
                                wv_ld = w_load(wvb, wv_d, h + 1, proj_done["v"])
                    v_ready = sv.n

                    for J in range(4):
                        npair = 2 * J + 2

                        def mmS(t, pair, par, half, h=h, J=J):
                            k = 2 * pair + half
                            return t.matmul(
                                psS[par][:, half * 512:half * 512 + 512],
                                k_h[:, k * 128:(k + 1) * 128],
                                q_h[:, J * 512:(J + 1) * 512],
                                start=True, stop=True, skip_group_check=True)

                        def emit_pair(pair):
                            par = pair % 2
                            w_ = [(sc, exp_consumed.get(par, 0)),
                                  (sv, v_ready), (sg, masks_done)]
                            marks = []
                            for half in range(2):
                                E.op("tensor",
                                     lambda t, pair=pair, par=par, half=half:
                                     mmS(t, pair, par, half),
                                     waits=w_ if half == 0 else (), inc=st)
                                marks.append(st.n)
                            need_mask = []
                            for half in range(2):
                                k = 2 * pair + half
                                m = k - 4 * J
                                if 0 <= m < 4:
                                    E.op("vector", lambda v, par=par, half=half,
                                         m=m: v.tensor_tensor(
                                             psS[par][:, half * 512:half * 512 + 512],
                                             psS[par][:, half * 512:half * 512 + 512],
                                             masks[m][:], OP.add),
                                         waits=[(st, marks[half])], inc=sv)
                                    need_mask.append(sv.n)
                            w_e = [(st, marks[1])]
                            if need_mask:
                                w_e.append((sv, need_mask[-1]))
                            w_e.append((st, pt_consumed.get(par, 0)))
                            E.op("scalar", lambda c_, par=par: c_.activation(
                                pt_sb[par][:], psS[par][:], AF.Exp,
                                scale=RSQRT_DH), waits=w_e, inc=sc)
                            exp_consumed[par] = sc.n
                            return sc.n

                        def emit_pv(pair, exp_mark, first, last):
                            par = pair % 2
                            for half in range(2):
                                k = 2 * pair + half
                                w_ = [(sc, exp_mark)]
                                if first and half == 0:
                                    w_ += [(sv, psO_free), (sv, psD_free)]
                                E.op("tensor", lambda t, k=k, par=par, half=half,
                                     first=first, last=last: t.matmul(
                                         psO[:],
                                         v_tm[:, k * 128:(k + 1) * 128],
                                         pt_sb[par][:, half * 512:half * 512 + 512],
                                         start=(first and half == 0),
                                         stop=(last and half == 1),
                                         skip_group_check=True),
                                     waits=w_)
                                E.op("tensor", lambda t, par=par, half=half,
                                     first=first, last=last: t.matmul(
                                         psD[:], ones_bf[:],
                                         pt_sb[par][:, half * 512:half * 512 + 512],
                                         start=(first and half == 0),
                                         stop=(last and half == 1),
                                         skip_group_check=True),
                                     inc=(st if last and half == 1 else None))
                            pt_consumed[par] = max(pt_consumed.get(par, 0), st.n) \
                                if (last or True) else pt_consumed.get(par, 0)

                        exp_marks = {}
                        exp_marks[0] = emit_pair(0)
                        for pair in range(npair):
                            if pair + 1 < npair:
                                exp_marks[pair + 1] = emit_pair(pair + 1)
                            emit_pv(pair, exp_marks[pair], first=(pair == 0),
                                    last=(pair == npair - 1))
                            pt_consumed[pair % 2] = st.n
                        pv_mark = st.n
                        E.op("vector", lambda v: v.reciprocal(rd_sb[:], psD[:]),
                             waits=[(st, pv_mark)], inc=sv)
                        rc_mark = sv.n
                        E.op("vector", lambda v: v.tensor_mul(u_sb[:], psO[:],
                                                              rd_sb[:]),
                             waits=[(sv, rc_mark)], inc=sv)
                        psO_free = sv.n
                        psD_free = sv.n
                        mu_mark = sv.n
                        E.op("vector", lambda v, h=h, J=J: v.tensor_scalar(
                            attn_h[:, h * T + J * 512:h * T + J * 512 + 512],
                            u_sb[:], bias_all[:, 32 + h:33 + h], None, OP.add),
                            waits=[(sv, mu_mark), (sv, bias_loaded)], inc=sv)
                        attn_written = sv.n
                attn_T_done = st.n
                attn_V_done = sv.n
                attn_C_done = sc.n
                attn_written = max(attn_written, 0)
            emit_tap("attn_h", attn_h, [(sv, attn_V_done)])
            b_gate_extra = so.n

        with ExitStack() as es_b:
            def sbb(name, shape, dt_):
                return es_b.enter_context(
                    nc.sbuf_tensor(name, shape, dt_, side="right"))
            act_e4 = sbb("act_e4", [128, NC_F * THALF], E4)
            w0e = [sbb(f"w0e{i}", [128, NC_D * 128], E4) for i in range(2)]
            w1e = [sbb(f"w1e{i}", [128, NC_D * 128], E4) for i in range(2)]
            w2e = [sbb(f"w2e{i}", [128, NC_F * 128], E4) for i in range(2)]
            wob = [sbb(f"wob{i}", [128, NC_D * 128], BF16) for i in range(1)]
            xg_sb = sbb("xg_sb", [128, THALF], BF16)
            f1_sb = sbb("f1_sb", [128, THALF], BF16)
            ta_sb = sbb("ta_sb", [128, THALF], BF16)
            tb_sb = sbb("tb_sb", [128, THALF], BF16)
            xcols = [sbb(f"xcols{i}", [128, 512], F32) for i in range(2)]
            ffn_sb = sbb("ffn_sb", [128, 512], BF16)
            s_sb = [sbb(f"s_sb{i}", [128, 512], F32) for i in range(2)]

            first_b_dma = [True]
            chain_done = {}
            g01_mark = {}
            psG_free = {0: 0, 1: 0}

            n_store = 0
            b2_done_v = 0
            b2_done_t = 0
            for half in range(2 if phases >= 3 else 0):
                with (nc.psum_tensor(f"psGa{half}", [128, 1024], F32) as psG0,
                      nc.psum_tensor(f"psGb{half}", [128, 1024], F32) as psG1,
                      nc.psum_tensor(f"psGc{half}", [128, 1024], F32) as psG2,
                      nc.psum_tensor(f"psGd{half}", [128, 1024], F32) as psG3):
                    psG = [[psG0, psG1], [psG2, psG3]]
                    for hc in range(NC_F):
                        buf = hc % 2
                        w_ = [(sc, chain_done.get(buf, 0))]
                        if first_b_dma[0]:
                            w_ += [(st, attn_T_done), (sv, attn_V_done),
                                   (sc, attn_C_done), (so, b_gate_extra)]
                            first_b_dma[0] = False
                        for mi, wsrc in enumerate((wg_d[0], wg_d[1])):
                            def f_w(g, wsrc=wsrc, hc=hc, buf=buf, mi=mi):
                                inap = wsrc.rearrange("(c p) m -> p c m", p=128)[
                                    :, :, hc * 128:(hc + 1) * 128]
                                dst = (w0e if mi == 0 else w1e)[buf]
                                oap = dst[:].rearrange("p (c m) -> p c m", m=128)
                                return g.dma_start(out=oap, in_=inap)
                            E.op("gpsimd", f_w, waits=w_ if mi == 0 else (),
                                 inc=sw, incn=16)
                        g01_mark[(half, hc)] = sw.n
                        mm_marks = []
                        for mi in range(2):
                            wt = (w0e if mi == 0 else w1e)[buf]
                            for tc2 in range(2):
                                for c in range(NC_D):
                                    w_mm = []
                                    if mi == 0 and tc2 == 0 and c == 0:
                                        w_mm = [(sw, g01_mark[(half, hc)]),
                                                (sc, chain_done.get(buf, 0)),
                                                (sv, b2_done_v),
                                                (st, b2_done_t)]
                                    E.op("tensor", lambda t, wt=wt, c=c, tc2=tc2,
                                         mi=mi, buf=buf, half=half: t.matmul(
                                             psG[buf][mi][:, tc2 * 512:
                                                          tc2 * 512 + 512],
                                             wt[:, c * 128:(c + 1) * 128],
                                             xn_e4[:, c * T + half * THALF +
                                                   tc2 * 512:
                                                   c * T + half * THALF +
                                                   tc2 * 512 + 512],
                                             start=(c == 0),
                                             stop=(c == NC_D - 1),
                                             skip_group_check=True),
                                         waits=w_mm,
                                         inc=(st if c == NC_D - 1 else None))
                                    if c == NC_D - 1:
                                        mm_marks.append(st.n)
                        gate_mark = mm_marks[1]
                        ff1_mark = mm_marks[3]
                        E.op("scalar", lambda c_, buf=buf: c_.activation(
                            xg_sb[:], psG[buf][0][:], AF.Identity),
                            waits=[(st, gate_mark)], inc=sc)
                        E.op("scalar", lambda c_, buf=buf: c_.activation(
                            f1_sb[:], psG[buf][1][:], AF.Identity),
                            waits=[(st, ff1_mark)], inc=sc)
                        xg_mark = sc.n
                        E.op("vector", lambda v: v.tensor_mul(
                            ta_sb[:], xg_sb[:], xg_sb[:]),
                            waits=[(sc, xg_mark)])
                        E.op("vector", lambda v: v.tensor_mul(
                            tb_sb[:], ta_sb[:], xg_sb[:]))
                        E.op("vector", lambda v: v.tensor_scalar(
                            ta_sb[:], tb_sb[:], C1, None, OP.mult))
                        E.op("vector", lambda v: v.tensor_add(
                            tb_sb[:], xg_sb[:], ta_sb[:]))
                        E.op("vector", lambda v: v.tensor_scalar(
                            ta_sb[:], tb_sb[:], C2, None, OP.mult), inc=sv)
                        t6_mark = sv.n
                        E.op("scalar", lambda c_: c_.activation(
                            tb_sb[:], ta_sb[:], AF.Tanh),
                            waits=[(sv, t6_mark)], inc=sc)
                        tanh_mark = sc.n
                        E.op("vector", lambda v: v.tensor_scalar(
                            ta_sb[:], tb_sb[:], 1.0, 0.5, OP.add, OP.mult),
                            waits=[(sc, tanh_mark)])
                        E.op("vector", lambda v: v.tensor_mul(
                            tb_sb[:], xg_sb[:], ta_sb[:]))
                        E.op("vector", lambda v: v.tensor_mul(
                            ta_sb[:], tb_sb[:], f1_sb[:]), inc=sv)
                        act_mark = sv.n
                        E.op("scalar", lambda c_, hc=hc: c_.copy(
                            act_e4[:, hc * THALF:(hc + 1) * THALF], ta_sb[:]),
                            waits=[(sv, act_mark)], inc=sc)
                        chain_done[buf] = sc.n
                    b1_done_c = sc.n
                    b1_done_t = st.n

                with (nc.psum_tensor(f"psF{half}", [128, 512], F32) as psF,
                      nc.psum_tensor(f"psA{half}", [128, 512], F32) as psA,
                      nc.psum_tensor(f"psTr{half}", [128, 512], F32) as psTT):
                    w2_mark = {}
                    wo_mark = {}
                    w2_used = {0: b2_done_t, 1: b2_done_t}
                    wo_used = {0: b2_done_t}
                    evac_used = {0: 0, 1: 0}
                    psF_free = 0
                    psA_free = 0
                    psTT_free = 0

                    def load_b2(dc):
                        buf = dc % 2
                        w_ = [(st, w2_used[buf]), (sc, b1_done_c)]
                        for piece in range(2):
                            def f_w2(g, dc=dc, buf=buf, piece=piece):
                                inap = wl_d.rearrange("(c p) m -> p c m", p=128)[
                                    :, piece * 32:piece * 32 + 32,
                                    dc * 128:(dc + 1) * 128]
                                oap = w2e[buf][:].rearrange("p (c m) -> p c m",
                                                            m=128)[
                                    :, piece * 32:piece * 32 + 32]
                                return g.dma_start(out=oap, in_=inap)
                            E.op("gpsimd", f_w2, waits=w_ if piece == 0 else (),
                                 inc=sw, incn=16)
                        w2_mark[dc] = sw.n

                    def load_wo(dc):
                        def f_wo(g, dc=dc):
                            inap = wo_d.rearrange("(c p) m -> p c m", p=128)[
                                :, :, dc * 128:(dc + 1) * 128]
                            oap = wob[0][:].rearrange("p (c m) -> p c m",
                                                      m=128)
                            return g.dma_start(out=oap, in_=inap)
                        E.op("gpsimd", f_wo,
                             waits=[(st, wo_used[0]), (sc, b1_done_c)],
                             inc=sw, incn=16)
                        wo_mark[dc] = sw.n

                    b2_it = [0]
                    pending = [None]
                    s_used = {0: 0, 1: 0}
                    s_store = {0: 0, 1: 0}
                    tail_state = {"n_store": n_store, "psTT_free": psTT_free,
                                  "add_mark": 0}

                    def emit_tail(dc, tc, tbase, xld, s_mark, sit):
                        for tb in range(4):
                            E.op("tensor", lambda t, tb=tb, sit=sit:
                                 t.transpose(
                                     psTT[:, tb * 128:(tb + 1) * 128],
                                     s_sb[sit % 2][:, tb * 128:(tb + 1) * 128],
                                     ident32[:]),
                                 waits=([(sv, s_mark),
                                         (sv, tail_state["add_mark"]),
                                         (st, tail_state["psTT_free"])]
                                        if tb == 0 else ()),
                                 inc=(st if tb == 3 else None))
                        tt_mark = st.n
                        s_used[sit % 2] = st.n
                        E.op("vector", lambda v, tc=tc, sit=sit: v.tensor_add(
                            s_sb[sit % 2][:], psTT[:], xcols[tc][:]),
                            waits=[(st, tt_mark), (sx, xld)], inc=sv)
                        tail_state["psTT_free"] = st.n
                        tail_state["add_mark"] = sv.n
                        evac_used[tc] = sv.n
                        st_mark = sv.n
                        def f_out(s, dc=dc, tbase=tbase, sit=sit):
                            oap = out_d.rearrange(
                                "(tt p) m -> p tt m", p=128)[
                                :, tbase // 128:tbase // 128 + 4,
                                dc * 128:(dc + 1) * 128]
                            iap = s_sb[sit % 2][:].rearrange(
                                "p (tt m) -> p tt m", m=128)
                            return s.dma_start(out=oap, in_=iap)
                        E.op("sync", f_out, waits=[(sv, st_mark)],
                             inc=so, incn=16)
                        s_store[sit % 2] = so.n
                        tail_state["n_store"] += 1

                    load_b2(0)
                    load_b2(1)
                    load_wo(0)
                    for dc in range(NC_D):
                        buf = dc % 2
                        for tc in range(2):
                            tbase = half * THALF + tc * 512
                            def f_x(s, dc=dc, tc=tc, tbase=tbase):
                                inap = x_d.rearrange(
                                    "(tt p) m -> p tt m", p=128)[
                                    :, tbase // 128:tbase // 128 + 4,
                                    dc * 128:(dc + 1) * 128]
                                oap = xcols[tc][:].rearrange(
                                    "p (tt m) -> p tt m", m=128)
                                return s.dma_start(out=oap, in_=inap)
                            xld = E.op("sync", f_x,
                                       waits=[(sv, evac_used[tc]),
                                              (st, attn_T_done),
                                              (sv, attn_V_done),
                                              (sc, attn_C_done)],
                                       inc=sx, incn=16)
                            for j in range(NC_F):
                                w_mm = []
                                if j == 0:
                                    w_mm = [(sw, w2_mark[dc]),
                                            (sv, psF_free),
                                            (st, b1_done_t)]
                                def f_dr(t, j=j, buf=buf, tc=tc):
                                    l_ = w2e[buf][:, j * 128:(j + 1) * 128]
                                    r_ = act_e4[:, j * THALF + tc * 512:
                                                j * THALF + tc * 512 + 512]
                                    return t.matmul(
                                        psF[:], l_, r_, start=(j == 0),
                                        stop=(j == NC_F - 1),
                                        skip_group_check=True)
                                E.op("tensor", f_dr, waits=w_mm,
                                     inc=(st if j == NC_F - 1 else None))
                            dr_mark = st.n
                            for c in range(NC_D):
                                w_mm = []
                                if c == 0:
                                    w_mm = [(sw, wo_mark[dc]),
                                            (sv, psA_free),
                                            (sv, attn_written)]
                                def f_op(t, c=c, tbase=tbase):
                                    return t.matmul(
                                        psA[:],
                                        wob[0][:, c * 128:(c + 1) * 128],
                                        attn_h[:, c * T + tbase:
                                               c * T + tbase + 512],
                                        start=(c == 0),
                                        stop=(c == NC_D - 1),
                                        skip_group_check=True)
                                E.op("tensor", f_op, waits=w_mm,
                                     inc=(st if c == NC_D - 1 else None))
                            op_mark = st.n
                            w2_used[buf] = st.n
                            wo_used[0] = st.n
                            sit = b2_it[0]
                            E.op("vector", lambda v: v.tensor_copy(
                                ffn_sb[:], psF[:]),
                                waits=[(st, dr_mark)], inc=sv)
                            psF_free = sv.n
                            E.op("vector", lambda v, dc=dc, sit=sit:
                                 v.scalar_tensor_tensor(
                                     s_sb[sit % 2][:], psA[:],
                                     bias_all[:, 48 + dc:49 + dc],
                                     ffn_sb[:], OP.add, OP.add),
                                 waits=[(st, op_mark),
                                        (st, s_used[sit % 2]),
                                        (so, s_store[sit % 2]),
                                        (sv, psF_free), (sv, bias_loaded)],
                                 inc=sv)
                            psA_free = sv.n
                            s_mark = sv.n
                            if pending[0] is not None:
                                emit_tail(*pending[0])
                            pending[0] = (dc, tc, tbase, xld, s_mark, sit)
                            b2_it[0] += 1
                        if dc + 2 < NC_D:
                            load_b2(dc + 2)
                        if dc + 1 < NC_D:
                            load_wo(dc + 1)
                    if pending[0] is not None:
                        emit_tail(*pending[0])
                        pending[0] = None
                    n_store = tail_state["n_store"]
                    psTT_free = tail_state["psTT_free"]
                    b2_done_v = sv.n
                    b2_done_t = st.n
                    b2_done_v = sv.n
                    b2_done_t = st.n

            E.op("sync", lambda s: s.nop(), waits=[(so, so.n)])

        with nc.Block() as blk:
            @blk.sync
            def _(eng):
                E.replay("sync", eng)

            @blk.gpsimd
            def _(eng):
                E.replay("gpsimd", eng)

            @blk.tensor
            def _(eng):
                E.replay("tensor", eng)

            @blk.vector
            def _(eng):
                E.replay("vector", eng)

            @blk.scalar
            def _(eng):
                E.replay("scalar", eng)

    return nc


def kernel(L0_attn_query, L0_attn_w_q, L0_attn_b_q, L0_attn_w_k, L0_attn_b_k,
           L0_attn_w_v, L0_attn_b_v, L0_attn_w_o, L0_attn_b_o,
           L0_ffn_w_gating, L0_ffn_w_linear):
    x = np.ascontiguousarray(np.asarray(L0_attn_query, dtype=np.float32))
    shared = {
        "wq": np.ascontiguousarray(L0_attn_w_q),
        "wk": np.ascontiguousarray(L0_attn_w_k),
        "wv": np.ascontiguousarray(L0_attn_w_v),
        "wo": np.ascontiguousarray(L0_attn_w_o),
        "bq": np.ascontiguousarray(L0_attn_b_q),
        "bk": np.ascontiguousarray(L0_attn_b_k),
        "bv": np.ascontiguousarray(L0_attn_b_v),
        "bo": np.ascontiguousarray(L0_attn_b_o),
        "wg": np.ascontiguousarray(L0_ffn_w_gating),
        "wl": np.ascontiguousarray(L0_ffn_w_linear),
    }
    shared["ident"] = np.eye(128, dtype=np.float32)
    shared["identb_in"] = np.eye(128, dtype=np.float32).astype(ml_dtypes.bfloat16)
    shared["ones_in"] = np.ones((128, 128), dtype=np.float32).astype(ml_dtypes.bfloat16)
    mk = np.zeros((4, 128, 512), np.float32)
    r = np.arange(128)[:, None]
    s = np.arange(512)[None, :]
    for m in range(4):
        mk[m] = np.where(s - r - 128 * m >= 0, 0.0, NEG).astype(np.float32)
    shared["maskc"] = mk
    nc = build_program()
    in_maps = [dict(shared, x=np.ascontiguousarray(x[b])) for b in range(8)]
    res = run_bass_kernel_spmd(nc, in_maps, core_ids=list(range(8)))
    return np.stack([r["out"] for r in res.results], axis=0)


# version 15
# speedup vs baseline: 1.0161x; 1.0161x over previous
import math
from contextlib import ExitStack

import numpy as np
import ml_dtypes

import concourse.bass as bass
import concourse.mybir as mybir
from concourse.bass_utils import run_bass_kernel_spmd

F32 = mybir.dt.float32
BF16 = mybir.dt.bfloat16
E4 = mybir.dt.float8e4
AF = mybir.ActivationFunctionType
OP = mybir.AluOpType
PM = mybir.MatmulPerfMode

T = 2048
D = 2048
NHEAD = 16
DH = 128
FF = 8192
NC_D = D // 128
NC_T = T // 128
NC_F = FF // 128
THALF = T // 2

C1 = float(np.float32(0.044715).astype(ml_dtypes.bfloat16))
C2 = float(np.float32(np.sqrt(2.0 / np.pi)).astype(ml_dtypes.bfloat16))
RSQRT_DH = 1.0 / math.sqrt(128.0)
EPS = 1e-5
NEG = -1.0e9

ENGINES = ("sync", "gpsimd", "tensor", "vector", "scalar")


class Ctr:

    def __init__(self, h):
        self.h = h
        self.n = 0


class Emitter:
    def __init__(self):
        self.streams = {e: [] for e in ENGINES}
        self.waited = {e: {} for e in ENGINES}

    def op(self, eng, fn, waits=(), inc=None, incn=1):
        w = []
        seen = self.waited[eng]
        for ctr, val in waits:
            if val is None:
                val = ctr.n
            key = id(ctr)
            if seen.get(key, -1) < val:
                seen[key] = val
                w.append((ctr.h, val))
        mark = None
        if inc is not None:
            inc.n += incn
            mark = inc.n
        self.streams[eng].append((fn, w, (inc.h if inc else None), incn))
        return mark

    def replay(self, eng_name, eng):
        for fn, waits, inc_h, incn in self.streams[eng_name]:
            for sem, val in waits:
                eng.wait_ge(sem, val)
            ins = fn(eng)
            if inc_h is not None:
                ins.then_inc(inc_h, incn)


def build_program(taps=(), phases=3, ablate_stats=False):
    nc = bass.Bass("TRN2", target_bir_lowering=False)
    x_d = nc.dram_tensor("x", [T, D], F32, kind="ExternalInput").ap()
    wq_d = nc.dram_tensor("wq", [D, D], F32, kind="ExternalInput").ap()
    wk_d = nc.dram_tensor("wk", [D, D], F32, kind="ExternalInput").ap()
    wv_d = nc.dram_tensor("wv", [D, D], F32, kind="ExternalInput").ap()
    wo_d = nc.dram_tensor("wo", [D, D], F32, kind="ExternalInput").ap()
    bq_d = nc.dram_tensor("bq", [D], F32, kind="ExternalInput").ap()
    bk_d = nc.dram_tensor("bk", [D], F32, kind="ExternalInput").ap()
    bv_d = nc.dram_tensor("bv", [D], F32, kind="ExternalInput").ap()
    bo_d = nc.dram_tensor("bo", [D], F32, kind="ExternalInput").ap()
    wg_d = nc.dram_tensor("wg", [2, D, FF], F32, kind="ExternalInput").ap()
    wl_d = nc.dram_tensor("wl", [FF, D], F32, kind="ExternalInput").ap()
    ident_d = nc.dram_tensor("ident", [128, 128], F32, kind="ExternalInput").ap()
    identb_d = nc.dram_tensor("identb_in", [128, 128], BF16, kind="ExternalInput").ap()
    ones_d = nc.dram_tensor("ones_in", [128, 128], BF16, kind="ExternalInput").ap()
    mask_d = nc.dram_tensor("maskc", [4, 128, 512], F32, kind="ExternalInput").ap()
    out_d = nc.dram_tensor("out", [T, D], F32, kind="ExternalOutput").ap()
    tap_outs = {}
    for name, shape, dt_ in taps:
        tap_outs[name] = nc.dram_tensor("tap_" + name, shape, dt_, kind="ExternalOutput").ap()

    E = Emitter()
    es = ExitStack()

    def sb(name, shape, dt_, side="left"):
        return es.enter_context(nc.sbuf_tensor(name, shape, dt_, side=side))

    def ps(name, shape, dt_=F32):
        return es.enter_context(nc.psum_tensor(name, shape, dt_))

    def sem(name):
        return Ctr(es.enter_context(nc.semaphore(name)))

    sx = sem("sx")
    sw = sem("sw")
    sg = sem("sg")
    st = sem("st")
    sv = sem("sv")
    sc = sem("sc")
    so = sem("so")
    sbm = sem("sbm")
    sx2 = sem("sx2")

    ident32 = sb("ident32", [128, 128], F32)
    bias_all = sb("bias_all", [128, 64], F32)
    xn_e4 = sb("xn_e4", [128, NC_D * T], E4)
    attn_h = sb("attn_h", [128, NHEAD * T], BF16)

    with es:
        marks = {}

        def init_phase():
            E.op("sync", lambda s: s.dma_start(out=ident32[:], in_=ident_d),
                 inc=sg, incn=16)
            E.op("sync", lambda s: s.dma_start(out=identb[:], in_=identb_d),
                 inc=sg, incn=16)
            E.op("sync", lambda s: s.dma_start(out=ones_bf[:], in_=ones_d),
                 inc=sg, incn=16)
            marks["ident32"] = sg.n
            marks["identb"] = sg.n

        init_phase()
        bias_loaded = None

        def init_masks():
            for m in range(4):
                E.op("gpsimd", lambda g, m=m: g.dma_start(
                    out=masks[m][:], in_=mask_d[m]),
                    inc=sg, incn=16)

        with ExitStack() as es_a:
            def sba(name, shape, dt_):
                return es_a.enter_context(
                    nc.sbuf_tensor(name, shape, dt_, side="right"))

            xn_bf = sba("xn_bf", [128, NC_D * T], BF16)
            ones_bf = sba("ones_bf", [128, 128], BF16)
            identb = sba("identb", [128, 128], BF16)
            bias_stage = sba("bias_stage", [64, 128], F32)
            masks = [sba(f"mask{m}", [128, 512], BF16) for m in range(4)]
            if phases >= 2:
                init_masks()
            masks_done = sg.n
            wqb = sba("wqb", [128, NC_D * 128], BF16)
            wkb = sba("wkb", [128, NC_D * 128], BF16)
            wvb = sba("wvb", [128, NC_D * 128], BF16)

            with ExitStack() as es0:
                def sb0(name, shape, dt_):
                    return es0.enter_context(
                        nc.sbuf_tensor(name, shape, dt_, side="right"))
                x_tm = [sb0(f"x_tm{i}", [128, D], F32) for i in range(2)]

                xn_tm = [sb0("xn_tm0", [128, D], F32) for i in range(1)]
                stat = [sb0(f"stat{i}", [128, 8], F32) for i in range(2)]
                with (nc.psum_tensor("psT0", [128, 512], F32) as psT0,
                      nc.psum_tensor("psT1", [128, 512], F32) as psT1):
                    psT = [psT0, psT1]
                    x_used = [0, 0]
                    xn_used = [0, 0]
                    evac_mark = {}
                    for i in range(NC_T):
                        b = i % 2
                        ld = E.op("sync", lambda s, i=i, b=b: s.dma_start(
                            out=x_tm[b][:], in_=x_d[i * 128:(i + 1) * 128, :]),
                            waits=[(sc, x_used[b]), (sx, sx.n)], inc=sx, incn=16)
                        ld2 = ld
                        E.op("vector", lambda v, b=b: v.reduce_sum(
                            stat[b][:, 0:1], x_tm[b][:], axis=mybir.AxisListType.X),
                            waits=[(sx, ld2)], inc=sv)
                        mk = sv.n
                        E.op("scalar", lambda c, b=b: c.activation(
                            xn_tm[0][:], x_tm[b][:], AF.Square),
                            waits=[(sx, ld2), (st, xn_used[0])], inc=sc)
                        sq_mark = sc.n
                        E.op("vector", lambda v, b=b: v.reduce_sum(
                            stat[b][:, 1:2], xn_tm[0][:],
                            axis=mybir.AxisListType.X),
                            waits=[(sc, sq_mark)], inc=sv)
                        sq_red = sv.n
                        E.op("vector", lambda v, b=b: v.tensor_scalar(
                            stat[b][:, 0:1], stat[b][:, 0:1], 1.0 / 2048.0, None,
                            OP.mult), waits=[(sv, mk)], inc=sv)
                        mk = sv.n
                        E.op("vector", lambda v, b=b: v.tensor_scalar(
                            stat[b][:, 1:2], stat[b][:, 1:2], 1.0 / 2048.0, None,
                            OP.mult), waits=[(sv, sq_red), (sv, mk)], inc=sv)
                        mk = sv.n
                        E.op("vector", lambda v, b=b: v.tensor_scalar(
                            stat[b][:, 2:3], stat[b][:, 0:1], stat[b][:, 0:1],
                            None, OP.mult), waits=[(sv, mk)], inc=sv)
                        mk = sv.n
                        E.op("vector", lambda v, b=b: v.tensor_scalar(
                            stat[b][:, 1:2], stat[b][:, 1:2], stat[b][:, 2:3],
                            EPS, OP.subtract, OP.add), waits=[(sv, mk)], inc=sv)
                        var_mark = sv.n
                        E.op("scalar", lambda c, b=b: c.activation(
                            stat[b][:, 3:4], stat[b][:, 1:2], AF.Sqrt),
                            waits=[(sv, var_mark)], inc=sc)
                        sqrt_mark = sc.n
                        E.op("vector", lambda v, b=b: v.reciprocal(
                            stat[b][:, 4:5], stat[b][:, 3:4]),
                            waits=[(sc, sqrt_mark)], inc=sv)
                        mk = sv.n
                        for _ in range(2):
                            E.op("vector", lambda v, b=b: v.tensor_scalar(
                                stat[b][:, 5:6], stat[b][:, 4:5], stat[b][:, 4:5],
                                None, OP.mult), waits=[(sv, mk)], inc=sv)
                            mk = sv.n
                            E.op("vector", lambda v, b=b: v.tensor_scalar(
                                stat[b][:, 5:6], stat[b][:, 5:6], stat[b][:, 1:2],
                                None, OP.mult), waits=[(sv, mk)], inc=sv)
                            mk = sv.n
                            E.op("vector", lambda v, b=b: v.tensor_scalar(
                                stat[b][:, 5:6], stat[b][:, 5:6], -0.5, 1.5,
                                OP.mult, OP.add), waits=[(sv, mk)], inc=sv)
                            mk = sv.n
                            E.op("vector", lambda v, b=b: v.tensor_scalar(
                                stat[b][:, 4:5], stat[b][:, 4:5], stat[b][:, 5:6],
                                None, OP.mult), waits=[(sv, mk)], inc=sv)
                            mk = sv.n
                        E.op("vector", lambda v, b=b: v.tensor_scalar(
                            stat[b][:, 6:7], stat[b][:, 0:1], stat[b][:, 4:5],
                            -1.0, OP.mult, OP.mult), waits=[(sv, mk)], inc=sv)
                        rstd_mark = sv.n
                        if ablate_stats:
                            E.op("scalar", lambda c, b=b: c.activation(
                                xn_tm[0][:], x_tm[b][:], AF.Identity,
                                bias=0.0, scale=1.0),
                                waits=[(sv, rstd_mark), (st, xn_used[0])],
                                inc=sc)
                        else:
                            E.op("scalar", lambda c, b=b: c.activation(
                                xn_tm[0][:], x_tm[b][:], AF.Identity,
                                bias=stat[b][:, 6:7], scale=stat[b][:, 4:5]),
                                waits=[(sv, rstd_mark), (st, xn_used[0])],
                                inc=sc)
                        xn_mark = sc.n
                        x_used[b] = sc.n
                        for jg in range(4):
                            pb = jg % 2
                            w_tr = [(sc, xn_mark), (sg, marks["ident32"])]
                            if (i, jg) in evac_mark or ((i * 4 + jg) >= 2):
                                prev = evac_mark.get((i, jg - 2) if jg >= 2
                                                    else (i - 1, jg + 2), 0)
                                w_tr.append((sv, prev))
                            for q in range(4):
                                cchunk = 4 * jg + q
                                E.op("tensor", lambda t, b=b, pb=pb, q=q,
                                     cchunk=cchunk: t.transpose(
                                         psT[pb][:, q * 128:(q + 1) * 128],
                                         xn_tm[0][:, cchunk * 128:(cchunk + 1) * 128],
                                         ident32[:]),
                                     waits=w_tr if q == 0 else (),
                                     inc=(st if q == 3 else None))
                            tr_mark = st.n
                            def f_evbf(v, i=i, jg=jg, pb=pb):
                                o = xn_bf[:].rearrange("p (c t) -> p c t", t=T)[
                                    :, 4 * jg:4 * jg + 4, i * 128:(i + 1) * 128]
                                s_ = psT[pb][:].rearrange("p (c t) -> p c t", t=128)
                                return v.tensor_copy(o, s_)
                            E.op("vector", f_evbf, waits=[(st, tr_mark)])
                            def f_ev4(v, i=i, jg=jg, pb=pb):
                                o = xn_e4[:].rearrange("p (c t) -> p c t", t=T)[
                                    :, 4 * jg:4 * jg + 4, i * 128:(i + 1) * 128]
                                s_ = psT[pb][:].rearrange("p (c t) -> p c t", t=128)
                                return v.tensor_copy(o, s_)
                            E.op("vector", f_ev4, inc=sv)
                            evac_mark[(i, jg)] = sv.n
                        xn_used[0] = st.n
                        marks[f"p0_tile{i}"] = sv.n
                        if i == 1:
                            marks["early_tiles"] = sv.n
                    phase0_v = sv.n
                    phase0_c = sc.n
                    for mi, bsrc in enumerate((bq_d, bk_d, bv_d, bo_d)):
                        def f_b(s, bsrc=bsrc, mi=mi):
                            return s.dma_start(
                                out=bias_stage[mi * 16:(mi + 1) * 16, :],
                                in_=bsrc.rearrange("(h p) -> h p", p=128))
                        E.op("sync", f_b, inc=sbm, incn=16)
                    with nc.psum_tensor("psBias", [128, 64], F32) as psB:
                        E.op("tensor", lambda t: t.transpose(
                            psB[:], bias_stage[:, :], ident32[0:64, 0:64]),
                            waits=[(sbm, sbm.n), (sg, marks["ident32"])], inc=st)
                        bias_tr = st.n
                        E.op("vector", lambda v: v.tensor_copy(
                            bias_all[:], psB[:]),
                            waits=[(st, bias_tr)], inc=sv)
                    bias_loaded = sv.n

            def emit_tap(name, tens, waits):
                if name in tap_outs:
                    E.op("sync", lambda s, tens=tens, name=name: s.dma_start(
                        out=tap_outs[name], in_=tens[:]), waits=waits,
                        inc=so, incn=16)
            emit_tap("xn_bf", xn_bf, [(sv, phase0_v), (sc, phase0_c)])
            emit_tap("xn_e4", xn_e4, [(sv, phase0_v), (sc, phase0_c)])

            q_h = sba("q_h", [128, T], BF16)
            k_h = sba("k_h", [128, T], BF16)
            v_tm = sba("v_tm", [128, NC_T * 128], BF16)
            v_tmp = sba("v_tmp", [128, 512], BF16)
            pt_sb = [sba(f"pt{j}", [128, 1024], BF16) for j in range(2)]
            rd_sb = sba("rd_sb", [128, 512], F32)
            u_sb = sba("u_sb", [128, 512], F32)

            with (nc.psum_tensor("psP", [128, 512], F32) as psP,
                  nc.psum_tensor("psVt", [128, 512], BF16) as psVt,
                  nc.psum_tensor("psS0", [128, 1024], F32) as psS0,
                  nc.psum_tensor("psS1", [128, 1024], F32) as psS1,
                  nc.psum_tensor("psO", [128, 512], F32) as psO,
                  nc.psum_tensor("psD", [128, 512], F32) as psD):
                psS = [psS0, psS1]

                def w_load(g_dst, src_d, h, wait_mark):
                    def f(g, g_dst=g_dst, src_d=src_d, h=h):
                        inap = src_d.rearrange("(c p) m -> p c m", p=128)[
                            :, :, h * 128:(h + 1) * 128]
                        oap = g_dst[:].rearrange("p (c m) -> p c m", m=128)
                        return g.dma_start(out=oap, in_=inap)
                    return E.op("gpsimd", f, waits=[(st, wait_mark)],
                                inc=sw, incn=16)

                wq_ld = w_load(wqb, wq_d, 0, 0)
                wk_ld = w_load(wkb, wk_d, 0, 0)
                wv_ld = w_load(wvb, wv_d, 0, 0)
                proj_done = {"q": 0, "k": 0, "v": 0}
                evacP_mark = 0
                psO_free = 0
                psD_free = 0
                exp_consumed = {}
                pt_consumed = {}
                attn_written = 0

                psVt_copy = 0
                for h in range(NHEAD if phases >= 2 else 0):
                    for kind, wbuf, wld_mark, bias in (
                            ("q", wqb, wq_ld, 0),
                            ("k", wkb, wk_ld, 16),
                            ("v", wvb, wv_ld, None)):
                        for tc in range(4):
                            for c in range(NC_D):
                                w_ = [(sw, wld_mark)] if c == 0 and tc == 0 else []
                                if c == 0:
                                    w_.append((sc, evacP_mark))
                                    w_.append(
                                        (sv, marks[f"p0_tile{4 * tc + 3}"]))
                                E.op("tensor", lambda t, wbuf=wbuf, c=c, tc=tc:
                                     t.matmul(psP[:],
                                              wbuf[:, c * 128:(c + 1) * 128],
                                              xn_bf[:, c * T + tc * 512:
                                                    c * T + tc * 512 + 512],
                                              start=(c == 0), stop=(c == NC_D - 1),
                                              skip_group_check=True),
                                     waits=w_,
                                     inc=(st if c == NC_D - 1 else None))
                                if c == NC_D - 1:
                                    mm_mark = st.n
                            if kind in ("q", "k"):
                                dest = q_h if kind == "q" else k_h
                                E.op("scalar", lambda c_, dest=dest, tc=tc,
                                     bias=bias, h=h: c_.activation(
                                         dest[:, tc * 512:(tc + 1) * 512], psP[:],
                                         AF.Identity, bias=bias_all[:, bias + h:bias + h + 1],
                                         scale=1.0),
                                     waits=[(st, mm_mark)], inc=sc)
                                evacP_mark = sc.n
                            else:
                                E.op("scalar", lambda c_, tc=tc: c_.activation(
                                    v_tmp[:], psP[:], AF.Identity),
                                    waits=[(st, mm_mark)], inc=sc)
                                evacP_mark = sc.n
                                vv_mark = sc.n
                                for tb in range(4):
                                    E.op("tensor", lambda t, tb=tb: t.transpose(
                                        psVt[:, tb * 128:(tb + 1) * 128],
                                        v_tmp[:, tb * 128:(tb + 1) * 128],
                                        identb[:]),
                                        waits=([(sc, vv_mark), (sv, psVt_copy),
                                                (sv, marks["identb"])]
                                               if tb == 0 else ()),
                                        inc=(st if tb == 3 else None))
                                vt_mark = st.n
                                E.op("vector", lambda v, tc=tc: v.tensor_copy(
                                    v_tm[:, tc * 512:(tc + 1) * 512], psVt[:]),
                                    waits=[(st, vt_mark)], inc=sv)
                                psVt_copy = sv.n
                        proj_done[kind] = st.n
                        if h + 1 < NHEAD:
                            if kind == "q":
                                wq_ld = w_load(wqb, wq_d, h + 1, proj_done["q"])
                            elif kind == "k":
                                wk_ld = w_load(wkb, wk_d, h + 1, proj_done["k"])
                            else:
                                wv_ld = w_load(wvb, wv_d, h + 1, proj_done["v"])
                    v_ready = sv.n

                    for J in range(4):
                        npair = 2 * J + 2

                        def mmS(t, pair, par, half, h=h, J=J):
                            k = 2 * pair + half
                            return t.matmul(
                                psS[par][:, half * 512:half * 512 + 512],
                                k_h[:, k * 128:(k + 1) * 128],
                                q_h[:, J * 512:(J + 1) * 512],
                                start=True, stop=True, skip_group_check=True)

                        def emit_pair(pair):
                            par = pair % 2
                            w_ = [(sc, exp_consumed.get(par, 0)),
                                  (sv, v_ready), (sg, masks_done)]
                            marks = []
                            for half in range(2):
                                E.op("tensor",
                                     lambda t, pair=pair, par=par, half=half:
                                     mmS(t, pair, par, half),
                                     waits=w_ if half == 0 else (), inc=st)
                                marks.append(st.n)
                            need_mask = []
                            for half in range(2):
                                k = 2 * pair + half
                                m = k - 4 * J
                                if 0 <= m < 4:
                                    E.op("vector", lambda v, par=par, half=half,
                                         m=m: v.tensor_tensor(
                                             psS[par][:, half * 512:half * 512 + 512],
                                             psS[par][:, half * 512:half * 512 + 512],
                                             masks[m][:], OP.add),
                                         waits=[(st, marks[half])], inc=sv)
                                    need_mask.append(sv.n)
                            w_e = [(st, marks[1])]
                            if need_mask:
                                w_e.append((sv, need_mask[-1]))
                            w_e.append((st, pt_consumed.get(par, 0)))
                            E.op("scalar", lambda c_, par=par: c_.activation(
                                pt_sb[par][:], psS[par][:], AF.Exp,
                                scale=RSQRT_DH), waits=w_e, inc=sc)
                            exp_consumed[par] = sc.n
                            return sc.n

                        def emit_pv(pair, exp_mark, first, last):
                            par = pair % 2
                            for half in range(2):
                                k = 2 * pair + half
                                w_ = [(sc, exp_mark)]
                                if first and half == 0:
                                    w_ += [(sv, psO_free), (sv, psD_free)]
                                E.op("tensor", lambda t, k=k, par=par, half=half,
                                     first=first, last=last: t.matmul(
                                         psO[:],
                                         v_tm[:, k * 128:(k + 1) * 128],
                                         pt_sb[par][:, half * 512:half * 512 + 512],
                                         start=(first and half == 0),
                                         stop=(last and half == 1),
                                         skip_group_check=True),
                                     waits=w_)
                                E.op("tensor", lambda t, par=par, half=half,
                                     first=first, last=last: t.matmul(
                                         psD[:], ones_bf[:],
                                         pt_sb[par][:, half * 512:half * 512 + 512],
                                         start=(first and half == 0),
                                         stop=(last and half == 1),
                                         skip_group_check=True),
                                     inc=(st if last and half == 1 else None))
                            pt_consumed[par] = max(pt_consumed.get(par, 0), st.n) \
                                if (last or True) else pt_consumed.get(par, 0)

                        exp_marks = {}
                        exp_marks[0] = emit_pair(0)
                        for pair in range(npair):
                            if pair + 1 < npair:
                                exp_marks[pair + 1] = emit_pair(pair + 1)
                            emit_pv(pair, exp_marks[pair], first=(pair == 0),
                                    last=(pair == npair - 1))
                            pt_consumed[pair % 2] = st.n
                        pv_mark = st.n
                        E.op("vector", lambda v: v.reciprocal(rd_sb[:], psD[:]),
                             waits=[(st, pv_mark)], inc=sv)
                        rc_mark = sv.n
                        E.op("vector", lambda v: v.tensor_mul(u_sb[:], psO[:],
                                                              rd_sb[:]),
                             waits=[(sv, rc_mark)], inc=sv)
                        psO_free = sv.n
                        psD_free = sv.n
                        mu_mark = sv.n
                        E.op("vector", lambda v, h=h, J=J: v.tensor_scalar(
                            attn_h[:, h * T + J * 512:h * T + J * 512 + 512],
                            u_sb[:], bias_all[:, 32 + h:33 + h], None, OP.add),
                            waits=[(sv, mu_mark), (sv, bias_loaded)], inc=sv)
                        attn_written = sv.n
                attn_T_done = st.n
                attn_V_done = sv.n
                attn_C_done = sc.n
                attn_written = max(attn_written, 0)
            emit_tap("attn_h", attn_h, [(sv, attn_V_done)])
            b_gate_extra = so.n

        with ExitStack() as es_b:
            def sbb(name, shape, dt_):
                return es_b.enter_context(
                    nc.sbuf_tensor(name, shape, dt_, side="right"))
            act_e4 = sbb("act_e4", [128, NC_F * THALF], E4)
            w0e = [sbb(f"w0e{i}", [128, NC_D * 128], E4) for i in range(2)]
            w1e = [sbb(f"w1e{i}", [128, NC_D * 128], E4) for i in range(2)]
            w2e = [sbb(f"w2e{i}", [128, NC_F * 128], E4) for i in range(2)]
            wob = [sbb(f"wob{i}", [128, NC_D * 128], BF16) for i in range(1)]
            xg_sb = sbb("xg_sb", [128, THALF], BF16)
            f1_sb = sbb("f1_sb", [128, THALF], BF16)
            ta_sb = sbb("ta_sb", [128, THALF], BF16)
            tb_sb = sbb("tb_sb", [128, THALF], BF16)
            xcols = [sbb(f"xcols{i}", [128, 512], F32) for i in range(2)]
            ffn_sb = sbb("ffn_sb", [128, 512], BF16)
            s_sb = [sbb(f"s_sb{i}", [128, 512], F32) for i in range(2)]

            first_b_dma = [True]
            chain_done = {}
            g01_mark = {}
            psG_free = {0: 0, 1: 0}

            n_store = 0
            b2_done_v = 0
            b2_done_t = 0
            for half in range(2 if phases >= 3 else 0):
                with (nc.psum_tensor(f"psGa{half}", [128, 1024], F32) as psG0,
                      nc.psum_tensor(f"psGb{half}", [128, 1024], F32) as psG1,
                      nc.psum_tensor(f"psGc{half}", [128, 1024], F32) as psG2,
                      nc.psum_tensor(f"psGd{half}", [128, 1024], F32) as psG3):
                    psG = [[psG0, psG1], [psG2, psG3]]
                    for hc in range(NC_F):
                        buf = hc % 2
                        w_ = [(sc, chain_done.get(buf, 0))]
                        if first_b_dma[0]:
                            w_ += [(st, attn_T_done), (sv, attn_V_done),
                                   (sc, attn_C_done), (so, b_gate_extra)]
                            first_b_dma[0] = False
                        for mi, wsrc in enumerate((wg_d[0], wg_d[1])):
                            def f_w(g, wsrc=wsrc, hc=hc, buf=buf, mi=mi):
                                inap = wsrc.rearrange("(c p) m -> p c m", p=128)[
                                    :, :, hc * 128:(hc + 1) * 128]
                                dst = (w0e if mi == 0 else w1e)[buf]
                                oap = dst[:].rearrange("p (c m) -> p c m", m=128)
                                return g.dma_start(out=oap, in_=inap)
                            E.op("gpsimd", f_w, waits=w_ if mi == 0 else (),
                                 inc=sw, incn=16)
                        g01_mark[(half, hc)] = sw.n
                        mm_marks = []
                        for mi in range(2):
                            wt = (w0e if mi == 0 else w1e)[buf]
                            for tc2 in range(2):
                                for c in range(NC_D):
                                    w_mm = []
                                    if mi == 0 and tc2 == 0 and c == 0:
                                        w_mm = [(sw, g01_mark[(half, hc)]),
                                                (sc, chain_done.get(buf, 0)),
                                                (sv, b2_done_v),
                                                (st, b2_done_t)]
                                    E.op("tensor", lambda t, wt=wt, c=c, tc2=tc2,
                                         mi=mi, buf=buf, half=half: t.matmul(
                                             psG[buf][mi][:, tc2 * 512:
                                                          tc2 * 512 + 512],
                                             wt[:, c * 128:(c + 1) * 128],
                                             xn_e4[:, c * T + half * THALF +
                                                   tc2 * 512:
                                                   c * T + half * THALF +
                                                   tc2 * 512 + 512],
                                             start=(c == 0),
                                             stop=(c == NC_D - 1),
                                             skip_group_check=True),
                                         waits=w_mm,
                                         inc=(st if c == NC_D - 1 else None))
                                    if c == NC_D - 1:
                                        mm_marks.append(st.n)
                        gate_mark = mm_marks[1]
                        ff1_mark = mm_marks[3]
                        E.op("scalar", lambda c_, buf=buf: c_.activation(
                            xg_sb[:], psG[buf][0][:], AF.Identity),
                            waits=[(st, gate_mark)], inc=sc)
                        E.op("scalar", lambda c_, buf=buf: c_.activation(
                            f1_sb[:], psG[buf][1][:], AF.Identity),
                            waits=[(st, ff1_mark)], inc=sc)
                        xg_mark = sc.n
                        E.op("vector", lambda v: v.tensor_mul(
                            ta_sb[:], xg_sb[:], xg_sb[:]),
                            waits=[(sc, xg_mark)])
                        E.op("vector", lambda v: v.tensor_mul(
                            tb_sb[:], ta_sb[:], xg_sb[:]))
                        E.op("vector", lambda v: v.tensor_scalar(
                            ta_sb[:], tb_sb[:], C1, None, OP.mult))
                        E.op("vector", lambda v: v.tensor_add(
                            tb_sb[:], xg_sb[:], ta_sb[:]))
                        E.op("vector", lambda v: v.tensor_scalar(
                            ta_sb[:], tb_sb[:], C2, None, OP.mult), inc=sv)
                        t6_mark = sv.n
                        E.op("scalar", lambda c_: c_.activation(
                            tb_sb[:], ta_sb[:], AF.Tanh),
                            waits=[(sv, t6_mark)], inc=sc)
                        tanh_mark = sc.n
                        E.op("vector", lambda v: v.tensor_scalar(
                            ta_sb[:], tb_sb[:], 1.0, 0.5, OP.add, OP.mult),
                            waits=[(sc, tanh_mark)])
                        E.op("vector", lambda v: v.tensor_mul(
                            tb_sb[:], xg_sb[:], ta_sb[:]))
                        E.op("vector", lambda v: v.tensor_mul(
                            ta_sb[:], tb_sb[:], f1_sb[:]), inc=sv)
                        act_mark = sv.n
                        E.op("scalar", lambda c_, hc=hc: c_.copy(
                            act_e4[:, hc * THALF:(hc + 1) * THALF], ta_sb[:]),
                            waits=[(sv, act_mark)], inc=sc)
                        chain_done[buf] = sc.n
                    b1_done_c = sc.n
                    b1_done_t = st.n

                with (nc.psum_tensor(f"psF{half}", [128, 512], F32) as psF,
                      nc.psum_tensor(f"psA{half}", [128, 512], F32) as psA,
                      nc.psum_tensor(f"psTr{half}", [128, 512], F32) as psTT):
                    w2_mark = {}
                    wo_mark = {}
                    w2_used = {0: b2_done_t, 1: b2_done_t}
                    wo_used = {0: b2_done_t}
                    evac_used = {0: 0, 1: 0}
                    psF_free = 0
                    psA_free = 0
                    psTT_free = 0

                    def load_b2(dc):
                        buf = dc % 2
                        w_ = [(st, w2_used[buf]), (sc, b1_done_c)]
                        for piece in range(2):
                            def f_w2(g, dc=dc, buf=buf, piece=piece):
                                inap = wl_d.rearrange("(c p) m -> p c m", p=128)[
                                    :, piece * 32:piece * 32 + 32,
                                    dc * 128:(dc + 1) * 128]
                                oap = w2e[buf][:].rearrange("p (c m) -> p c m",
                                                            m=128)[
                                    :, piece * 32:piece * 32 + 32]
                                return g.dma_start(out=oap, in_=inap)
                            E.op("gpsimd", f_w2, waits=w_ if piece == 0 else (),
                                 inc=sw, incn=16)
                        w2_mark[dc] = sw.n

                    def load_wo(dc):
                        def f_wo(g, dc=dc):
                            inap = wo_d.rearrange("(c p) m -> p c m", p=128)[
                                :, :, dc * 128:(dc + 1) * 128]
                            oap = wob[0][:].rearrange("p (c m) -> p c m",
                                                      m=128)
                            return g.dma_start(out=oap, in_=inap)
                        E.op("gpsimd", f_wo,
                             waits=[(st, wo_used[0]), (sc, b1_done_c)],
                             inc=sw, incn=16)
                        wo_mark[dc] = sw.n

                    b2_it = [0]
                    pending = [None]
                    s_used = {0: 0, 1: 0}
                    s_store = {0: 0, 1: 0}
                    tail_state = {"n_store": n_store, "psTT_free": psTT_free,
                                  "add_mark": 0}

                    def emit_tail(dc, tc, tbase, xld, s_mark, sit):
                        for tb in range(4):
                            E.op("tensor", lambda t, tb=tb, sit=sit:
                                 t.transpose(
                                     psTT[:, tb * 128:(tb + 1) * 128],
                                     s_sb[sit % 2][:, tb * 128:(tb + 1) * 128],
                                     ident32[:]),
                                 waits=([(sv, s_mark),
                                         (sv, tail_state["add_mark"]),
                                         (st, tail_state["psTT_free"])]
                                        if tb == 0 else ()),
                                 inc=(st if tb == 3 else None))
                        tt_mark = st.n
                        s_used[sit % 2] = st.n
                        E.op("vector", lambda v, tc=tc, sit=sit: v.tensor_add(
                            s_sb[sit % 2][:], psTT[:], xcols[tc][:]),
                            waits=[(st, tt_mark), (sx, xld)], inc=sv)
                        tail_state["psTT_free"] = st.n
                        tail_state["add_mark"] = sv.n
                        evac_used[tc] = sv.n
                        st_mark = sv.n
                        def f_out(s, dc=dc, tbase=tbase, sit=sit):
                            oap = out_d.rearrange(
                                "(tt p) m -> p tt m", p=128)[
                                :, tbase // 128:tbase // 128 + 4,
                                dc * 128:(dc + 1) * 128]
                            iap = s_sb[sit % 2][:].rearrange(
                                "p (tt m) -> p tt m", m=128)
                            return s.dma_start(out=oap, in_=iap)
                        E.op("sync", f_out, waits=[(sv, st_mark)],
                             inc=so, incn=16)
                        s_store[sit % 2] = so.n
                        tail_state["n_store"] += 1

                    load_b2(0)
                    load_b2(1)
                    load_wo(0)
                    for dc in range(NC_D):
                        buf = dc % 2
                        for tc in range(2):
                            tbase = half * THALF + tc * 512
                            def f_x(s, dc=dc, tc=tc, tbase=tbase):
                                inap = x_d.rearrange(
                                    "(tt p) m -> p tt m", p=128)[
                                    :, tbase // 128:tbase // 128 + 4,
                                    dc * 128:(dc + 1) * 128]
                                oap = xcols[tc][:].rearrange(
                                    "p (tt m) -> p tt m", m=128)
                                return s.dma_start(out=oap, in_=inap)
                            xld = E.op("sync", f_x,
                                       waits=[(sv, evac_used[tc]),
                                              (st, attn_T_done),
                                              (sv, attn_V_done),
                                              (sc, attn_C_done)],
                                       inc=sx, incn=16)
                            for j in range(NC_F):
                                w_mm = []
                                if j == 0:
                                    w_mm = [(sw, w2_mark[dc]),
                                            (sv, psF_free),
                                            (st, b1_done_t)]
                                def f_dr(t, j=j, buf=buf, tc=tc):
                                    l_ = w2e[buf][:, j * 128:(j + 1) * 128]
                                    r_ = act_e4[:, j * THALF + tc * 512:
                                                j * THALF + tc * 512 + 512]
                                    return t.matmul(
                                        psF[:], l_, r_, start=(j == 0),
                                        stop=(j == NC_F - 1),
                                        skip_group_check=True)
                                E.op("tensor", f_dr, waits=w_mm,
                                     inc=(st if j == NC_F - 1 else None))
                            dr_mark = st.n
                            for c in range(NC_D):
                                w_mm = []
                                if c == 0:
                                    w_mm = [(sw, wo_mark[dc]),
                                            (sv, psA_free),
                                            (sv, attn_written)]
                                def f_op(t, c=c, tbase=tbase):
                                    return t.matmul(
                                        psA[:],
                                        wob[0][:, c * 128:(c + 1) * 128],
                                        attn_h[:, c * T + tbase:
                                               c * T + tbase + 512],
                                        start=(c == 0),
                                        stop=(c == NC_D - 1),
                                        skip_group_check=True)
                                E.op("tensor", f_op, waits=w_mm,
                                     inc=(st if c == NC_D - 1 else None))
                            op_mark = st.n
                            w2_used[buf] = st.n
                            wo_used[0] = st.n
                            sit = b2_it[0]
                            E.op("vector", lambda v: v.tensor_copy(
                                ffn_sb[:], psF[:]),
                                waits=[(st, dr_mark)], inc=sv)
                            psF_free = sv.n
                            E.op("vector", lambda v, dc=dc, sit=sit:
                                 v.scalar_tensor_tensor(
                                     s_sb[sit % 2][:], psA[:],
                                     bias_all[:, 48 + dc:49 + dc],
                                     ffn_sb[:], OP.add, OP.add),
                                 waits=[(st, op_mark),
                                        (st, s_used[sit % 2]),
                                        (so, s_store[sit % 2]),
                                        (sv, psF_free), (sv, bias_loaded)],
                                 inc=sv)
                            psA_free = sv.n
                            s_mark = sv.n
                            if pending[0] is not None:
                                emit_tail(*pending[0])
                            pending[0] = (dc, tc, tbase, xld, s_mark, sit)
                            b2_it[0] += 1
                        if dc + 2 < NC_D:
                            load_b2(dc + 2)
                        if dc + 1 < NC_D:
                            load_wo(dc + 1)
                    if pending[0] is not None:
                        emit_tail(*pending[0])
                        pending[0] = None
                    n_store = tail_state["n_store"]
                    psTT_free = tail_state["psTT_free"]
                    b2_done_v = sv.n
                    b2_done_t = st.n
                    b2_done_v = sv.n
                    b2_done_t = st.n

            E.op("sync", lambda s: s.nop(), waits=[(so, so.n)])

        with nc.Block() as blk:
            @blk.sync
            def _(eng):
                E.replay("sync", eng)

            @blk.gpsimd
            def _(eng):
                E.replay("gpsimd", eng)

            @blk.tensor
            def _(eng):
                E.replay("tensor", eng)

            @blk.vector
            def _(eng):
                E.replay("vector", eng)

            @blk.scalar
            def _(eng):
                E.replay("scalar", eng)

    return nc


def kernel(L0_attn_query, L0_attn_w_q, L0_attn_b_q, L0_attn_w_k, L0_attn_b_k,
           L0_attn_w_v, L0_attn_b_v, L0_attn_w_o, L0_attn_b_o,
           L0_ffn_w_gating, L0_ffn_w_linear):
    x = np.ascontiguousarray(np.asarray(L0_attn_query, dtype=np.float32))
    shared = {
        "wq": np.ascontiguousarray(L0_attn_w_q),
        "wk": np.ascontiguousarray(L0_attn_w_k),
        "wv": np.ascontiguousarray(L0_attn_w_v),
        "wo": np.ascontiguousarray(L0_attn_w_o),
        "bq": np.ascontiguousarray(L0_attn_b_q),
        "bk": np.ascontiguousarray(L0_attn_b_k),
        "bv": np.ascontiguousarray(L0_attn_b_v),
        "bo": np.ascontiguousarray(L0_attn_b_o),
        "wg": np.ascontiguousarray(L0_ffn_w_gating),
        "wl": np.ascontiguousarray(L0_ffn_w_linear),
    }
    shared["ident"] = np.eye(128, dtype=np.float32)
    shared["identb_in"] = np.eye(128, dtype=np.float32).astype(ml_dtypes.bfloat16)
    shared["ones_in"] = np.ones((128, 128), dtype=np.float32).astype(ml_dtypes.bfloat16)
    mk = np.zeros((4, 128, 512), np.float32)
    r = np.arange(128)[:, None]
    s = np.arange(512)[None, :]
    for m in range(4):
        mk[m] = np.where(s - r - 128 * m >= 0, 0.0, NEG).astype(np.float32)
    shared["maskc"] = mk
    nc = build_program()
    in_maps = [dict(shared, x=np.ascontiguousarray(x[b])) for b in range(8)]
    res = run_bass_kernel_spmd(nc, in_maps, core_ids=list(range(8)))
    return np.stack([r["out"] for r in res.results], axis=0)


# version 16 (final)
# speedup vs baseline: 1.0638x; 1.0469x over previous
import math
from contextlib import ExitStack

import numpy as np
import ml_dtypes

import concourse.bass as bass
import concourse.mybir as mybir
from concourse.bass_utils import run_bass_kernel_spmd

F32 = mybir.dt.float32
BF16 = mybir.dt.bfloat16
E4 = mybir.dt.float8e4
AF = mybir.ActivationFunctionType
OP = mybir.AluOpType
PM = mybir.MatmulPerfMode

T = 2048
D = 2048
NHEAD = 16
DH = 128
FF = 8192
NC_D = D // 128
NC_T = T // 128
NC_F = FF // 128
THALF = T // 2

C1 = float(np.float32(0.044715).astype(ml_dtypes.bfloat16))
C2 = float(np.float32(np.sqrt(2.0 / np.pi)).astype(ml_dtypes.bfloat16))
RSQRT_DH = 1.0 / math.sqrt(128.0)
EPS = 1e-5
NEG = -1.0e9

ENGINES = ("sync", "gpsimd", "tensor", "vector", "scalar")


class Ctr:

    def __init__(self, h):
        self.h = h
        self.n = 0


class Emitter:
    def __init__(self):
        self.streams = {e: [] for e in ENGINES}
        self.waited = {e: {} for e in ENGINES}

    def op(self, eng, fn, waits=(), inc=None, incn=1):
        w = []
        seen = self.waited[eng]
        for ctr, val in waits:
            if val is None:
                val = ctr.n
            key = id(ctr)
            if seen.get(key, -1) < val:
                seen[key] = val
                w.append((ctr.h, val))
        mark = None
        if inc is not None:
            inc.n += incn
            mark = inc.n
        self.streams[eng].append((fn, w, (inc.h if inc else None), incn))
        return mark

    def replay(self, eng_name, eng):
        for fn, waits, inc_h, incn in self.streams[eng_name]:
            for sem, val in waits:
                eng.wait_ge(sem, val)
            ins = fn(eng)
            if inc_h is not None:
                ins.then_inc(inc_h, incn)


def build_program(taps=(), phases=3, ablate_stats=False):
    nc = bass.Bass("TRN2", target_bir_lowering=False)
    x_d = nc.dram_tensor("x", [T, D], F32, kind="ExternalInput").ap()
    wq_d = nc.dram_tensor("wq", [D, D], F32, kind="ExternalInput").ap()
    wk_d = nc.dram_tensor("wk", [D, D], F32, kind="ExternalInput").ap()
    wv_d = nc.dram_tensor("wv", [D, D], F32, kind="ExternalInput").ap()
    wo_d = nc.dram_tensor("wo", [D, D], F32, kind="ExternalInput").ap()
    bq_d = nc.dram_tensor("bq", [D], F32, kind="ExternalInput").ap()
    bk_d = nc.dram_tensor("bk", [D], F32, kind="ExternalInput").ap()
    bv_d = nc.dram_tensor("bv", [D], F32, kind="ExternalInput").ap()
    bo_d = nc.dram_tensor("bo", [D], F32, kind="ExternalInput").ap()
    wg_d = nc.dram_tensor("wg", [2, D, FF], F32, kind="ExternalInput").ap()
    wl_d = nc.dram_tensor("wl", [FF, D], F32, kind="ExternalInput").ap()
    ident_d = nc.dram_tensor("ident", [128, 128], F32, kind="ExternalInput").ap()
    identb_d = nc.dram_tensor("identb_in", [128, 128], BF16, kind="ExternalInput").ap()
    ones_d = nc.dram_tensor("ones_in", [128, 128], BF16, kind="ExternalInput").ap()
    mask_d = nc.dram_tensor("maskc", [4, 128, 512], F32, kind="ExternalInput").ap()
    out_d = nc.dram_tensor("out", [T, D], F32, kind="ExternalOutput").ap()
    tap_outs = {}
    for name, shape, dt_ in taps:
        tap_outs[name] = nc.dram_tensor("tap_" + name, shape, dt_, kind="ExternalOutput").ap()

    E = Emitter()
    es = ExitStack()

    def sb(name, shape, dt_, side="left"):
        return es.enter_context(nc.sbuf_tensor(name, shape, dt_, side=side))

    def ps(name, shape, dt_=F32):
        return es.enter_context(nc.psum_tensor(name, shape, dt_))

    def sem(name):
        return Ctr(es.enter_context(nc.semaphore(name)))

    sx = sem("sx")
    sw = sem("sw")
    sg = sem("sg")
    st = sem("st")
    sv = sem("sv")
    sc = sem("sc")
    so = sem("so")
    sbm = sem("sbm")
    sx2 = sem("sx2")

    ident32 = sb("ident32", [128, 128], F32)
    bias_all = sb("bias_all", [128, 64], F32)
    xn_e4 = sb("xn_e4", [128, NC_D * T], E4)
    attn_h = sb("attn_h", [128, NHEAD * T], BF16)

    with es:
        marks = {}

        def init_phase():
            E.op("sync", lambda s: s.dma_start(out=ident32[:], in_=ident_d),
                 inc=sg, incn=16)
            E.op("sync", lambda s: s.dma_start(out=identb[:], in_=identb_d),
                 inc=sg, incn=16)
            E.op("sync", lambda s: s.dma_start(out=ones_bf[:], in_=ones_d),
                 inc=sg, incn=16)
            marks["ident32"] = sg.n
            marks["identb"] = sg.n

        init_phase()
        bias_loaded = None

        def init_masks():
            for m in range(4):
                E.op("gpsimd", lambda g, m=m: g.dma_start(
                    out=masks[m][:], in_=mask_d[m]),
                    inc=sg, incn=16)

        with ExitStack() as es_a:
            def sba(name, shape, dt_):
                return es_a.enter_context(
                    nc.sbuf_tensor(name, shape, dt_, side="right"))

            xn_bf = sba("xn_bf", [128, NC_D * T], BF16)
            ones_bf = sba("ones_bf", [128, 128], BF16)
            identb = sba("identb", [128, 128], BF16)
            bias_stage = sba("bias_stage", [64, 128], F32)
            masks = [sba(f"mask{m}", [128, 512], BF16) for m in range(4)]
            if phases >= 2:
                init_masks()
            masks_done = sg.n
            wqb = sba("wqb", [128, NC_D * 128], BF16)
            wkb = sba("wkb", [128, NC_D * 128], BF16)
            wvb = sba("wvb", [128, NC_D * 128], BF16)

            with ExitStack() as es0:
                def sb0(name, shape, dt_):
                    return es0.enter_context(
                        nc.sbuf_tensor(name, shape, dt_, side="right"))
                x_tm = [sb0(f"x_tm{i}", [128, D], F32) for i in range(2)]

                xn_tm = [sb0("xn_tm0", [128, D], F32) for i in range(1)]
                stat = [sb0(f"stat{i}", [128, 8], F32) for i in range(2)]
                with (nc.psum_tensor("psT0", [128, 512], F32) as psT0,
                      nc.psum_tensor("psT1", [128, 512], F32) as psT1):
                    psT = [psT0, psT1]
                    x_used = [0, 0]
                    xn_used = [0, 0]
                    evac_mark = {}
                    for i in range(NC_T):
                        b = i % 2
                        ld = E.op("sync", lambda s, i=i, b=b: s.dma_start(
                            out=x_tm[b][:], in_=x_d[i * 128:(i + 1) * 128, :]),
                            waits=[(sc, x_used[b]), (sx, sx.n)], inc=sx, incn=16)
                        ld2 = ld
                        E.op("vector", lambda v, b=b: v.reduce_sum(
                            stat[b][:, 0:1], x_tm[b][:], axis=mybir.AxisListType.X),
                            waits=[(sx, ld2)], inc=sv)
                        mk = sv.n
                        E.op("scalar", lambda c, b=b: c.activation(
                            xn_tm[0][:], x_tm[b][:], AF.Square),
                            waits=[(sx, ld2), (st, xn_used[0])], inc=sc)
                        sq_mark = sc.n
                        E.op("vector", lambda v, b=b: v.reduce_sum(
                            stat[b][:, 1:2], xn_tm[0][:],
                            axis=mybir.AxisListType.X),
                            waits=[(sc, sq_mark)], inc=sv)
                        sq_red = sv.n
                        E.op("vector", lambda v, b=b: v.tensor_scalar(
                            stat[b][:, 0:1], stat[b][:, 0:1], 1.0 / 2048.0, None,
                            OP.mult), waits=[(sv, mk)], inc=sv)
                        mk = sv.n
                        E.op("vector", lambda v, b=b: v.tensor_scalar(
                            stat[b][:, 1:2], stat[b][:, 1:2], 1.0 / 2048.0, None,
                            OP.mult), waits=[(sv, sq_red), (sv, mk)], inc=sv)
                        mk = sv.n
                        E.op("vector", lambda v, b=b: v.tensor_scalar(
                            stat[b][:, 2:3], stat[b][:, 0:1], stat[b][:, 0:1],
                            None, OP.mult), waits=[(sv, mk)], inc=sv)
                        mk = sv.n
                        E.op("vector", lambda v, b=b: v.tensor_scalar(
                            stat[b][:, 1:2], stat[b][:, 1:2], stat[b][:, 2:3],
                            EPS, OP.subtract, OP.add), waits=[(sv, mk)], inc=sv)
                        var_mark = sv.n
                        E.op("scalar", lambda c, b=b: c.activation(
                            stat[b][:, 3:4], stat[b][:, 1:2], AF.Sqrt),
                            waits=[(sv, var_mark)], inc=sc)
                        sqrt_mark = sc.n
                        E.op("vector", lambda v, b=b: v.reciprocal(
                            stat[b][:, 4:5], stat[b][:, 3:4]),
                            waits=[(sc, sqrt_mark)], inc=sv)
                        mk = sv.n
                        for _ in range(2):
                            E.op("vector", lambda v, b=b: v.tensor_scalar(
                                stat[b][:, 5:6], stat[b][:, 4:5], stat[b][:, 4:5],
                                None, OP.mult), waits=[(sv, mk)], inc=sv)
                            mk = sv.n
                            E.op("vector", lambda v, b=b: v.tensor_scalar(
                                stat[b][:, 5:6], stat[b][:, 5:6], stat[b][:, 1:2],
                                None, OP.mult), waits=[(sv, mk)], inc=sv)
                            mk = sv.n
                            E.op("vector", lambda v, b=b: v.tensor_scalar(
                                stat[b][:, 5:6], stat[b][:, 5:6], -0.5, 1.5,
                                OP.mult, OP.add), waits=[(sv, mk)], inc=sv)
                            mk = sv.n
                            E.op("vector", lambda v, b=b: v.tensor_scalar(
                                stat[b][:, 4:5], stat[b][:, 4:5], stat[b][:, 5:6],
                                None, OP.mult), waits=[(sv, mk)], inc=sv)
                            mk = sv.n
                        E.op("vector", lambda v, b=b: v.tensor_scalar(
                            stat[b][:, 6:7], stat[b][:, 0:1], stat[b][:, 4:5],
                            -1.0, OP.mult, OP.mult), waits=[(sv, mk)], inc=sv)
                        rstd_mark = sv.n
                        if ablate_stats:
                            E.op("scalar", lambda c, b=b: c.activation(
                                xn_tm[0][:], x_tm[b][:], AF.Identity,
                                bias=0.0, scale=1.0),
                                waits=[(sv, rstd_mark), (st, xn_used[0])],
                                inc=sc)
                        else:
                            E.op("scalar", lambda c, b=b: c.activation(
                                xn_tm[0][:], x_tm[b][:], AF.Identity,
                                bias=stat[b][:, 6:7], scale=stat[b][:, 4:5]),
                                waits=[(sv, rstd_mark), (st, xn_used[0])],
                                inc=sc)
                        xn_mark = sc.n
                        x_used[b] = sc.n
                        for jg in range(4):
                            pb = jg % 2
                            w_tr = [(sc, xn_mark), (sg, marks["ident32"])]
                            if (i, jg) in evac_mark or ((i * 4 + jg) >= 2):
                                prev = evac_mark.get((i, jg - 2) if jg >= 2
                                                    else (i - 1, jg + 2), 0)
                                w_tr.append((sv, prev))
                            for q in range(4):
                                cchunk = 4 * jg + q
                                E.op("tensor", lambda t, b=b, pb=pb, q=q,
                                     cchunk=cchunk: t.transpose(
                                         psT[pb][:, q * 128:(q + 1) * 128],
                                         xn_tm[0][:, cchunk * 128:(cchunk + 1) * 128],
                                         ident32[:]),
                                     waits=w_tr if q == 0 else (),
                                     inc=(st if q == 3 else None))
                            tr_mark = st.n
                            def f_evbf(v, i=i, jg=jg, pb=pb):
                                o = xn_bf[:].rearrange("p (c t) -> p c t", t=T)[
                                    :, 4 * jg:4 * jg + 4, i * 128:(i + 1) * 128]
                                s_ = psT[pb][:].rearrange("p (c t) -> p c t", t=128)
                                return v.tensor_copy(o, s_)
                            E.op("vector", f_evbf, waits=[(st, tr_mark)])
                            def f_ev4(v, i=i, jg=jg, pb=pb):
                                o = xn_e4[:].rearrange("p (c t) -> p c t", t=T)[
                                    :, 4 * jg:4 * jg + 4, i * 128:(i + 1) * 128]
                                s_ = psT[pb][:].rearrange("p (c t) -> p c t", t=128)
                                return v.tensor_copy(o, s_)
                            E.op("vector", f_ev4, inc=sv)
                            evac_mark[(i, jg)] = sv.n
                        xn_used[0] = st.n
                        marks[f"p0_tile{i}"] = sv.n
                        if i == 1:
                            marks["early_tiles"] = sv.n
                    phase0_v = sv.n
                    phase0_c = sc.n
                    for mi, bsrc in enumerate((bq_d, bk_d, bv_d, bo_d)):
                        def f_b(s, bsrc=bsrc, mi=mi):
                            return s.dma_start(
                                out=bias_stage[mi * 16:(mi + 1) * 16, :],
                                in_=bsrc.rearrange("(h p) -> h p", p=128))
                        E.op("sync", f_b, inc=sbm, incn=16)
                    with nc.psum_tensor("psBias", [128, 64], F32) as psB:
                        E.op("tensor", lambda t: t.transpose(
                            psB[:], bias_stage[:, :], ident32[0:64, 0:64]),
                            waits=[(sbm, sbm.n), (sg, marks["ident32"])], inc=st)
                        bias_tr = st.n
                        E.op("vector", lambda v: v.tensor_copy(
                            bias_all[:], psB[:]),
                            waits=[(st, bias_tr)], inc=sv)
                    bias_loaded = sv.n

            def emit_tap(name, tens, waits):
                if name in tap_outs:
                    E.op("sync", lambda s, tens=tens, name=name: s.dma_start(
                        out=tap_outs[name], in_=tens[:]), waits=waits,
                        inc=so, incn=16)
            emit_tap("xn_bf", xn_bf, [(sv, phase0_v), (sc, phase0_c)])
            emit_tap("xn_e4", xn_e4, [(sv, phase0_v), (sc, phase0_c)])

            q_h = sba("q_h", [128, T], BF16)
            k_h = sba("k_h", [128, T], BF16)
            v_tm = sba("v_tm", [128, NC_T * 128], BF16)
            v_tmp = sba("v_tmp", [128, 512], BF16)
            pt_sb = [sba(f"pt{j}", [128, 1024], BF16) for j in range(2)]
            rd_sb = sba("rd_sb", [128, 512], F32)
            u_sb = sba("u_sb", [128, 512], F32)

            with (nc.psum_tensor("psP", [128, 512], F32) as psP,
                  nc.psum_tensor("psVt", [128, 512], BF16) as psVt,
                  nc.psum_tensor("psS0", [128, 1024], F32) as psS0,
                  nc.psum_tensor("psS1", [128, 1024], F32) as psS1,
                  nc.psum_tensor("psO", [128, 512], F32) as psO,
                  nc.psum_tensor("psD", [128, 512], F32) as psD):
                psS = [psS0, psS1]

                def w_load(g_dst, src_d, h, wait_mark):
                    def f(g, g_dst=g_dst, src_d=src_d, h=h):
                        inap = src_d.rearrange("(c p) m -> p c m", p=128)[
                            :, :, h * 128:(h + 1) * 128]
                        oap = g_dst[:].rearrange("p (c m) -> p c m", m=128)
                        return g.dma_start(out=oap, in_=inap)
                    return E.op("gpsimd", f, waits=[(st, wait_mark)],
                                inc=sw, incn=16)

                wq_ld = w_load(wqb, wq_d, 0, 0)
                wk_ld = w_load(wkb, wk_d, 0, 0)
                wv_ld = w_load(wvb, wv_d, 0, 0)
                proj_done = {"q": 0, "k": 0, "v": 0}
                evacP_mark = 0
                psO_free = 0
                psD_free = 0
                exp_consumed = {}
                pt_consumed = {}
                attn_written = 0

                psVt_copy = 0
                for h in range(NHEAD if phases >= 2 else 0):
                    for kind, wbuf, wld_mark, bias in (
                            ("q", wqb, wq_ld, 0),
                            ("k", wkb, wk_ld, 16),
                            ("v", wvb, wv_ld, None)):
                        for tc in range(4):
                            for c in range(NC_D):
                                w_ = [(sw, wld_mark)] if c == 0 and tc == 0 else []
                                if c == 0:
                                    w_.append((sc, evacP_mark))
                                    w_.append(
                                        (sv, marks[f"p0_tile{4 * tc + 3}"]))
                                E.op("tensor", lambda t, wbuf=wbuf, c=c, tc=tc:
                                     t.matmul(psP[:],
                                              wbuf[:, c * 128:(c + 1) * 128],
                                              xn_bf[:, c * T + tc * 512:
                                                    c * T + tc * 512 + 512],
                                              start=(c == 0), stop=(c == NC_D - 1),
                                              skip_group_check=True),
                                     waits=w_,
                                     inc=(st if c == NC_D - 1 else None))
                                if c == NC_D - 1:
                                    mm_mark = st.n
                            if kind in ("q", "k"):
                                dest = q_h if kind == "q" else k_h
                                E.op("scalar", lambda c_, dest=dest, tc=tc,
                                     bias=bias, h=h: c_.activation(
                                         dest[:, tc * 512:(tc + 1) * 512], psP[:],
                                         AF.Identity, bias=bias_all[:, bias + h:bias + h + 1],
                                         scale=1.0),
                                     waits=[(st, mm_mark)], inc=sc)
                                evacP_mark = sc.n
                            else:
                                E.op("scalar", lambda c_, tc=tc: c_.activation(
                                    v_tmp[:], psP[:], AF.Identity),
                                    waits=[(st, mm_mark)], inc=sc)
                                evacP_mark = sc.n
                                vv_mark = sc.n
                                for tb in range(4):
                                    E.op("tensor", lambda t, tb=tb: t.transpose(
                                        psVt[:, tb * 128:(tb + 1) * 128],
                                        v_tmp[:, tb * 128:(tb + 1) * 128],
                                        identb[:]),
                                        waits=([(sc, vv_mark), (sv, psVt_copy),
                                                (sv, marks["identb"])]
                                               if tb == 0 else ()),
                                        inc=(st if tb == 3 else None))
                                vt_mark = st.n
                                E.op("vector", lambda v, tc=tc: v.tensor_copy(
                                    v_tm[:, tc * 512:(tc + 1) * 512], psVt[:]),
                                    waits=[(st, vt_mark)], inc=sv)
                                psVt_copy = sv.n
                        proj_done[kind] = st.n
                        if h + 1 < NHEAD:
                            if kind == "q":
                                wq_ld = w_load(wqb, wq_d, h + 1, proj_done["q"])
                            elif kind == "k":
                                wk_ld = w_load(wkb, wk_d, h + 1, proj_done["k"])
                            else:
                                wv_ld = w_load(wvb, wv_d, h + 1, proj_done["v"])
                    v_ready = sv.n

                    for J in range(4):
                        npair = 2 * J + 2

                        def mmS(t, pair, par, half, h=h, J=J):
                            k = 2 * pair + half
                            return t.matmul(
                                psS[par][:, half * 512:half * 512 + 512],
                                k_h[:, k * 128:(k + 1) * 128],
                                q_h[:, J * 512:(J + 1) * 512],
                                start=True, stop=True, skip_group_check=True)

                        def emit_pair(pair):
                            par = pair % 2
                            w_ = [(sc, exp_consumed.get(par, 0)),
                                  (sv, v_ready), (sg, masks_done)]
                            marks = []
                            for half in range(2):
                                E.op("tensor",
                                     lambda t, pair=pair, par=par, half=half:
                                     mmS(t, pair, par, half),
                                     waits=w_ if half == 0 else (), inc=st)
                                marks.append(st.n)
                            need_mask = []
                            for half in range(2):
                                k = 2 * pair + half
                                m = k - 4 * J
                                if 0 <= m < 4:
                                    E.op("vector", lambda v, par=par, half=half,
                                         m=m: v.tensor_tensor(
                                             psS[par][:, half * 512:half * 512 + 512],
                                             psS[par][:, half * 512:half * 512 + 512],
                                             masks[m][:], OP.add),
                                         waits=[(st, marks[half])], inc=sv)
                                    need_mask.append(sv.n)
                            w_e = [(st, marks[1])]
                            if need_mask:
                                w_e.append((sv, need_mask[-1]))
                            w_e.append((st, pt_consumed.get(par, 0)))
                            E.op("scalar", lambda c_, par=par: c_.activation(
                                pt_sb[par][:], psS[par][:], AF.Exp,
                                scale=RSQRT_DH), waits=w_e, inc=sc)
                            exp_consumed[par] = sc.n
                            return sc.n

                        def emit_pv(pair, exp_mark, first, last):
                            par = pair % 2
                            for half in range(2):
                                k = 2 * pair + half
                                w_ = [(sc, exp_mark)]
                                if first and half == 0:
                                    w_ += [(sv, psO_free), (sv, psD_free)]
                                E.op("tensor", lambda t, k=k, par=par, half=half,
                                     first=first, last=last: t.matmul(
                                         psO[:],
                                         v_tm[:, k * 128:(k + 1) * 128],
                                         pt_sb[par][:, half * 512:half * 512 + 512],
                                         start=(first and half == 0),
                                         stop=(last and half == 1),
                                         skip_group_check=True),
                                     waits=w_)
                                E.op("tensor", lambda t, par=par, half=half,
                                     first=first, last=last: t.matmul(
                                         psD[:], ones_bf[:],
                                         pt_sb[par][:, half * 512:half * 512 + 512],
                                         start=(first and half == 0),
                                         stop=(last and half == 1),
                                         skip_group_check=True),
                                     inc=(st if last and half == 1 else None))
                            pt_consumed[par] = max(pt_consumed.get(par, 0), st.n) \
                                if (last or True) else pt_consumed.get(par, 0)

                        exp_marks = {}
                        exp_marks[0] = emit_pair(0)
                        for pair in range(npair):
                            if pair + 1 < npair:
                                exp_marks[pair + 1] = emit_pair(pair + 1)
                            emit_pv(pair, exp_marks[pair], first=(pair == 0),
                                    last=(pair == npair - 1))
                            pt_consumed[pair % 2] = st.n
                        pv_mark = st.n
                        E.op("vector", lambda v: v.reciprocal(rd_sb[:], psD[:]),
                             waits=[(st, pv_mark)], inc=sv)
                        rc_mark = sv.n
                        E.op("vector", lambda v: v.tensor_mul(u_sb[:], psO[:],
                                                              rd_sb[:]),
                             waits=[(sv, rc_mark)], inc=sv)
                        psO_free = sv.n
                        psD_free = sv.n
                        mu_mark = sv.n
                        E.op("vector", lambda v, h=h, J=J: v.tensor_scalar(
                            attn_h[:, h * T + J * 512:h * T + J * 512 + 512],
                            u_sb[:], bias_all[:, 32 + h:33 + h], None, OP.add),
                            waits=[(sv, mu_mark), (sv, bias_loaded)], inc=sv)
                        attn_written = sv.n
                attn_T_done = st.n
                attn_V_done = sv.n
                attn_C_done = sc.n
                attn_written = max(attn_written, 0)
            emit_tap("attn_h", attn_h, [(sv, attn_V_done)])
            b_gate_extra = so.n

        with ExitStack() as es_b:
            def sbb(name, shape, dt_):
                return es_b.enter_context(
                    nc.sbuf_tensor(name, shape, dt_, side="right"))
            act_e4 = sbb("act_e4", [128, NC_F * THALF], E4)
            w0e = [sbb(f"w0e{i}", [128, NC_D * 128], E4) for i in range(2)]
            w1e = [sbb(f"w1e{i}", [128, NC_D * 128], E4) for i in range(2)]
            w2e = [sbb(f"w2e{i}", [128, NC_F * 128], E4) for i in range(2)]
            wob = [sbb(f"wob{i}", [128, NC_D * 128], BF16) for i in range(1)]
            xg_sb = sbb("xg_sb", [128, THALF], BF16)
            f1_sb = sbb("f1_sb", [128, THALF], BF16)
            ta_sb = sbb("ta_sb", [128, THALF], BF16)
            tb_sb = sbb("tb_sb", [128, THALF], BF16)
            xcols = [sbb(f"xcols{i}", [128, 512], F32) for i in range(2)]
            ffn_sb = sbb("ffn_sb", [128, 512], BF16)
            s_sb = [sbb(f"s_sb{i}", [128, 512], F32) for i in range(2)]

            first_b_dma = [True]
            chain_done = {}
            g01_mark = {}
            psG_free = {0: 0, 1: 0}

            n_store = 0
            b2_done_v = 0
            b2_done_t = 0
            for half in range(2 if phases >= 3 else 0):
                with (nc.psum_tensor(f"psGa{half}", [128, 1024], F32) as psG0,
                      nc.psum_tensor(f"psGb{half}", [128, 1024], F32) as psG1,
                      nc.psum_tensor(f"psGc{half}", [128, 1024], F32) as psG2,
                      nc.psum_tensor(f"psGd{half}", [128, 1024], F32) as psG3):
                    psG = [[psG0, psG1], [psG2, psG3]]
                    for hc in range(NC_F):
                        buf = hc % 2
                        w_ = [(sc, chain_done.get(buf, 0))]
                        if first_b_dma[0]:
                            w_ += [(st, attn_T_done), (sv, attn_V_done),
                                   (sc, attn_C_done), (so, b_gate_extra)]
                            first_b_dma[0] = False
                        for mi, wsrc in enumerate((wg_d[0], wg_d[1])):
                            def f_w(g, wsrc=wsrc, hc=hc, buf=buf, mi=mi):
                                inap = wsrc.rearrange("(c p) m -> p c m", p=128)[
                                    :, :, hc * 128:(hc + 1) * 128]
                                dst = (w0e if mi == 0 else w1e)[buf]
                                oap = dst[:].rearrange("p (c m) -> p c m", m=128)
                                return g.dma_start(out=oap, in_=inap)
                            E.op("gpsimd", f_w, waits=w_ if mi == 0 else (),
                                 inc=sw, incn=16)
                        g01_mark[(half, hc)] = sw.n
                        mm_marks = []
                        for mi in range(2):
                            wt = (w0e if mi == 0 else w1e)[buf]
                            for tc2 in range(2):
                                for c in range(NC_D):
                                    w_mm = []
                                    if mi == 0 and tc2 == 0 and c == 0:
                                        w_mm = [(sw, g01_mark[(half, hc)]),
                                                (sc, chain_done.get(buf, 0)),
                                                (sv, b2_done_v),
                                                (st, b2_done_t)]
                                    E.op("tensor", lambda t, wt=wt, c=c, tc2=tc2,
                                         mi=mi, buf=buf, half=half: t.matmul(
                                             psG[buf][mi][:, tc2 * 512:
                                                          tc2 * 512 + 512],
                                             wt[:, c * 128:(c + 1) * 128],
                                             xn_e4[:, c * T + half * THALF +
                                                   tc2 * 512:
                                                   c * T + half * THALF +
                                                   tc2 * 512 + 512],
                                             start=(c == 0),
                                             stop=(c == NC_D - 1),
                                             skip_group_check=True),
                                         waits=w_mm,
                                         inc=(st if c == NC_D - 1 else None))
                                    if c == NC_D - 1:
                                        mm_marks.append(st.n)
                        gate_mark = mm_marks[1]
                        ff1_mark = mm_marks[3]
                        E.op("scalar", lambda c_, buf=buf: c_.activation(
                            xg_sb[:], psG[buf][0][:], AF.Identity),
                            waits=[(st, gate_mark)], inc=sc)
                        E.op("scalar", lambda c_, buf=buf: c_.activation(
                            f1_sb[:], psG[buf][1][:], AF.Identity),
                            waits=[(st, ff1_mark)], inc=sc)
                        xg_mark = sc.n
                        E.op("vector", lambda v: v.tensor_mul(
                            ta_sb[:], xg_sb[:], xg_sb[:]),
                            waits=[(sc, xg_mark)])
                        E.op("vector", lambda v: v.tensor_mul(
                            tb_sb[:], ta_sb[:], xg_sb[:]))
                        E.op("vector", lambda v: v.tensor_scalar(
                            ta_sb[:], tb_sb[:], C1, None, OP.mult))
                        E.op("vector", lambda v: v.tensor_add(
                            tb_sb[:], xg_sb[:], ta_sb[:]))
                        E.op("vector", lambda v: v.tensor_scalar(
                            ta_sb[:], tb_sb[:], C2, None, OP.mult), inc=sv)
                        t6_mark = sv.n
                        E.op("scalar", lambda c_: c_.activation(
                            tb_sb[:], ta_sb[:], AF.Tanh),
                            waits=[(sv, t6_mark)], inc=sc)
                        tanh_mark = sc.n
                        E.op("vector", lambda v: v.tensor_scalar(
                            ta_sb[:], tb_sb[:], 1.0, 0.5, OP.add, OP.mult),
                            waits=[(sc, tanh_mark)])
                        E.op("vector", lambda v: v.tensor_mul(
                            tb_sb[:], xg_sb[:], ta_sb[:]))
                        E.op("vector", lambda v: v.tensor_mul(
                            ta_sb[:], tb_sb[:], f1_sb[:]), inc=sv)
                        act_mark = sv.n
                        E.op("scalar", lambda c_, hc=hc: c_.copy(
                            act_e4[:, hc * THALF:(hc + 1) * THALF], ta_sb[:]),
                            waits=[(sv, act_mark)], inc=sc)
                        chain_done[buf] = sc.n
                    b1_done_c = sc.n
                    b1_done_t = st.n

                with (nc.psum_tensor(f"psF{half}", [128, 512], F32) as psF,
                      nc.psum_tensor(f"psA{half}", [128, 512], F32) as psA,
                      nc.psum_tensor(f"psTr{half}", [128, 512], F32) as psTT):
                    w2_mark = {}
                    wo_mark = {}
                    w2_used = {0: b2_done_t, 1: b2_done_t}
                    wo_used = {0: b2_done_t}
                    evac_used = {0: 0, 1: 0}
                    psF_free = 0
                    psA_free = 0
                    psTT_free = 0

                    def load_b2(dc):
                        buf = dc % 2
                        w_ = [(st, w2_used[buf]), (sc, b1_done_c)]
                        for piece in range(2):
                            def f_w2(g, dc=dc, buf=buf, piece=piece):
                                inap = wl_d.rearrange("(c p) m -> p c m", p=128)[
                                    :, piece * 32:piece * 32 + 32,
                                    dc * 128:(dc + 1) * 128]
                                oap = w2e[buf][:].rearrange("p (c m) -> p c m",
                                                            m=128)[
                                    :, piece * 32:piece * 32 + 32]
                                return g.dma_start(out=oap, in_=inap)
                            E.op("gpsimd", f_w2, waits=w_ if piece == 0 else (),
                                 inc=sw, incn=16)
                        w2_mark[dc] = sw.n

                    def load_wo(dc):
                        def f_wo(g, dc=dc):
                            inap = wo_d.rearrange("(c p) m -> p c m", p=128)[
                                :, :, dc * 128:(dc + 1) * 128]
                            oap = wob[0][:].rearrange("p (c m) -> p c m",
                                                      m=128)
                            return g.dma_start(out=oap, in_=inap)
                        E.op("gpsimd", f_wo,
                             waits=[(st, wo_used[0]), (sc, b1_done_c)],
                             inc=sw, incn=16)
                        wo_mark[dc] = sw.n

                    b2_it = [0]
                    pending = [None]
                    s_used = {0: 0, 1: 0}
                    s_store = {0: 0, 1: 0}
                    tail_state = {"n_store": n_store, "psTT_free": psTT_free,
                                  "add_mark": 0}

                    def emit_tail(dc, tc, tbase, xld, s_mark, sit):
                        for tb in range(4):
                            E.op("tensor", lambda t, tb=tb, sit=sit:
                                 t.transpose(
                                     psTT[:, tb * 128:(tb + 1) * 128],
                                     s_sb[sit % 2][:, tb * 128:(tb + 1) * 128],
                                     ident32[:]),
                                 waits=([(sv, s_mark),
                                         (sv, tail_state["add_mark"]),
                                         (st, tail_state["psTT_free"])]
                                        if tb == 0 else ()),
                                 inc=(st if tb == 3 else None))
                        tt_mark = st.n
                        s_used[sit % 2] = st.n
                        E.op("vector", lambda v, tc=tc, sit=sit: v.tensor_add(
                            s_sb[sit % 2][:], psTT[:], xcols[tc][:]),
                            waits=[(st, tt_mark), (sx, xld)], inc=sv)
                        tail_state["psTT_free"] = st.n
                        tail_state["add_mark"] = sv.n
                        evac_used[tc] = sv.n
                        st_mark = sv.n
                        def f_out(s, dc=dc, tbase=tbase, sit=sit):
                            oap = out_d.rearrange(
                                "(tt p) m -> p tt m", p=128)[
                                :, tbase // 128:tbase // 128 + 4,
                                dc * 128:(dc + 1) * 128]
                            iap = s_sb[sit % 2][:].rearrange(
                                "p (tt m) -> p tt m", m=128)
                            return s.dma_start(out=oap, in_=iap)
                        E.op("sync", f_out, waits=[(sv, st_mark)],
                             inc=so, incn=16)
                        s_store[sit % 2] = so.n
                        tail_state["n_store"] += 1

                    load_b2(0)
                    load_b2(1)
                    load_wo(0)
                    for dc in range(NC_D):
                        buf = dc % 2
                        for tc in range(2):
                            tbase = half * THALF + tc * 512
                            def f_x(s, dc=dc, tc=tc, tbase=tbase):
                                inap = x_d.rearrange(
                                    "(tt p) m -> p tt m", p=128)[
                                    :, tbase // 128:tbase // 128 + 4,
                                    dc * 128:(dc + 1) * 128]
                                oap = xcols[tc][:].rearrange(
                                    "p (tt m) -> p tt m", m=128)
                                return s.dma_start(out=oap, in_=inap)
                            xld = E.op("sync", f_x,
                                       waits=[(sv, evac_used[tc]),
                                              (st, attn_T_done),
                                              (sv, attn_V_done),
                                              (sc, attn_C_done)],
                                       inc=sx, incn=16)
                            for j in range(NC_F // 2):
                                w_mm = []
                                if j == 0:
                                    w_mm = [(sw, w2_mark[dc]),
                                            (sv, psF_free),
                                            (st, b1_done_t)]
                                def f_dr(t, j=j, buf=buf, tc=tc):
                                    l_ = w2e[buf][:].rearrange(
                                        "p (j i m) -> p j i m", i=2, m=128)[
                                        :, j]
                                    r_ = act_e4[:].rearrange(
                                        "p (j i t) -> p j i t", i=2,
                                        t=THALF)[:, j, :,
                                                 tc * 512:tc * 512 + 512]
                                    return t.matmul(
                                        psF[:], l_, r_, start=(j == 0),
                                        stop=(j == NC_F // 2 - 1),
                                        perf_mode=PM.DoubleRow,
                                        skip_group_check=True)
                                E.op("tensor", f_dr, waits=w_mm,
                                     inc=(st if j == NC_F // 2 - 1 else None))
                            dr_mark = st.n
                            for c in range(NC_D):
                                w_mm = []
                                if c == 0:
                                    w_mm = [(sw, wo_mark[dc]),
                                            (sv, psA_free),
                                            (sv, attn_written)]
                                def f_op(t, c=c, tbase=tbase):
                                    return t.matmul(
                                        psA[:],
                                        wob[0][:, c * 128:(c + 1) * 128],
                                        attn_h[:, c * T + tbase:
                                               c * T + tbase + 512],
                                        start=(c == 0),
                                        stop=(c == NC_D - 1),
                                        skip_group_check=True)
                                E.op("tensor", f_op, waits=w_mm,
                                     inc=(st if c == NC_D - 1 else None))
                            op_mark = st.n
                            w2_used[buf] = st.n
                            wo_used[0] = st.n
                            sit = b2_it[0]
                            E.op("vector", lambda v: v.tensor_copy(
                                ffn_sb[:], psF[:]),
                                waits=[(st, dr_mark)], inc=sv)
                            psF_free = sv.n
                            E.op("vector", lambda v, dc=dc, sit=sit:
                                 v.scalar_tensor_tensor(
                                     s_sb[sit % 2][:], psA[:],
                                     bias_all[:, 48 + dc:49 + dc],
                                     ffn_sb[:], OP.add, OP.add),
                                 waits=[(st, op_mark),
                                        (st, s_used[sit % 2]),
                                        (so, s_store[sit % 2]),
                                        (sv, psF_free), (sv, bias_loaded)],
                                 inc=sv)
                            psA_free = sv.n
                            s_mark = sv.n
                            if pending[0] is not None:
                                emit_tail(*pending[0])
                            pending[0] = (dc, tc, tbase, xld, s_mark, sit)
                            b2_it[0] += 1
                        if dc + 2 < NC_D:
                            load_b2(dc + 2)
                        if dc + 1 < NC_D:
                            load_wo(dc + 1)
                    if pending[0] is not None:
                        emit_tail(*pending[0])
                        pending[0] = None
                    n_store = tail_state["n_store"]
                    psTT_free = tail_state["psTT_free"]
                    b2_done_v = sv.n
                    b2_done_t = st.n
                    b2_done_v = sv.n
                    b2_done_t = st.n

            E.op("sync", lambda s: s.nop(), waits=[(so, so.n)])

        with nc.Block() as blk:
            @blk.sync
            def _(eng):
                E.replay("sync", eng)

            @blk.gpsimd
            def _(eng):
                E.replay("gpsimd", eng)

            @blk.tensor
            def _(eng):
                E.replay("tensor", eng)

            @blk.vector
            def _(eng):
                E.replay("vector", eng)

            @blk.scalar
            def _(eng):
                E.replay("scalar", eng)

    return nc


def kernel(L0_attn_query, L0_attn_w_q, L0_attn_b_q, L0_attn_w_k, L0_attn_b_k,
           L0_attn_w_v, L0_attn_b_v, L0_attn_w_o, L0_attn_b_o,
           L0_ffn_w_gating, L0_ffn_w_linear):
    x = np.ascontiguousarray(np.asarray(L0_attn_query, dtype=np.float32))
    shared = {
        "wq": np.ascontiguousarray(L0_attn_w_q),
        "wk": np.ascontiguousarray(L0_attn_w_k),
        "wv": np.ascontiguousarray(L0_attn_w_v),
        "wo": np.ascontiguousarray(L0_attn_w_o),
        "bq": np.ascontiguousarray(L0_attn_b_q),
        "bk": np.ascontiguousarray(L0_attn_b_k),
        "bv": np.ascontiguousarray(L0_attn_b_v),
        "bo": np.ascontiguousarray(L0_attn_b_o),
        "wg": np.ascontiguousarray(L0_ffn_w_gating),
        "wl": np.ascontiguousarray(L0_ffn_w_linear),
    }
    shared["ident"] = np.eye(128, dtype=np.float32)
    shared["identb_in"] = np.eye(128, dtype=np.float32).astype(ml_dtypes.bfloat16)
    shared["ones_in"] = np.ones((128, 128), dtype=np.float32).astype(ml_dtypes.bfloat16)
    mk = np.zeros((4, 128, 512), np.float32)
    r = np.arange(128)[:, None]
    s = np.arange(512)[None, :]
    for m in range(4):
        mk[m] = np.where(s - r - 128 * m >= 0, 0.0, NEG).astype(np.float32)
    shared["maskc"] = mk
    nc = build_program()
    in_maps = [dict(shared, x=np.ascontiguousarray(x[b])) for b in range(8)]
    res = run_bass_kernel_spmd(nc, in_maps, core_ids=list(range(8)))
    return np.stack([r["out"] for r in res.results], axis=0)
